# version 13
# baseline (speedup 1.0000x reference)
import numpy as np
import ml_dtypes
from contextlib import ExitStack

import concourse.bass as bass
import concourse.mybir as mybir
from concourse.bass_utils import run_bass_kernel_spmd

F32 = mybir.dt.float32
BF16 = mybir.dt.bfloat16
AF = mybir.ActivationFunctionType
ALU = mybir.AluOpType
AX = mybir.AxisListType
BF = ml_dtypes.bfloat16

TOK = 2048
PW = 2176
OW0 = 4
D = 1024
H = 8
DH = 128
FEAT = 64
E = 8
FF = 2048
NC = 8
BISECT = 32
CAP = 2048

XN = [c * PW for c in range(8)]
MASK0 = 17408
EYEB = MASK0 + 2560
ONES128B = EYEB + 128
ONES1B = ONES128B + 16
PROJB = ONES1B + 128
SWG = PROJB + 64
ESEL8 = SWG + 64
BFDUM = ESEL8 + 1024
RA = BFDUM + 2048
WHEAD = RA
WQKP = RA + 3072
TRB = RA + 4096
VTM = RA + 4352
SST = RA + 6528
QPHI = RA + 9088
KPHI = RA + 11264
OALL = RA + 13440
A_END = OALL + 16384
W1B = RA
W2B = RA + 8192
ASB = RA + 16384
GBE = RA + 24576
B_END = GBE + 2048
NBF = max(A_END, B_END)

EYEF = 0
ONES1F = 128
BOUTT = 256
HM = 264
SELOWN = 280
PAT864 = 288
GRP648 = 352
GRP864 = 360
EYEE = 424
TILE648 = 432
STATS = 440
SMALL = 584
LO8, HI8, TH8, CNT64, ISGE, ISLT, TMP8, TH64 = (SMALL + i for i in range(8))
EPSC = SMALL + 8
CNT8 = SMALL + 9
PEM = 600
MPR = PEM + 2048
RMX = MPR + 2048
BIG = RMX + 2048
OACC = [BIG + c * 2048 for c in range(8)]
XT = [BIG, BIG + 1024]
SQD = BIG + 2048
AGC = BIG + 3072
SCR = BIG + 5120
LGT = BIG + 7168
SCA = BIG + 9216
SCB = BIG + 9728
ZSC = BIG + 10240
ZBS = BIG + 10752
D8S = BIG + 11264
BIGEND = BIG + 16384
XRL = [BIGEND, BIGEND + 1024]
GSC = BIGEND + 2048
NFP = BIGEND + 2560

GELU_BF = True
DEBUG_DUMPS = False
STOP_AFTER = None
HEAD_STAGE = None
GLVL = 9
SERIAL = 0


class SemC:
    def __init__(self, h):
        self.h = h
        self.n = 0

    def inc(self, k=1):
        self.n += k
        return self.n


def build_nc():
    nc = bass.Bass("TRN2", target_bir_lowering=False, debug=False,
                   detect_race_conditions=False)
    dt = nc.dram_tensor
    xm = dt("xm", [TOK, D], F32, kind="ExternalInput").ap()
    xhl = dt("xhl", [3, D], F32, kind="ExternalInput").ap()
    xhr = dt("xhr", [3, D], F32, kind="ExternalInput").ap()
    wqkv = dt("wqkv", [D, 3 * D], F32, kind="ExternalInput").ap()
    proj = dt("proj", [DH, FEAT], F32, kind="ExternalInput").ap()
    wout = dt("wout", [D, D], F32, kind="ExternalInput").ap()
    boutT = dt("boutT", [128, 8], F32, kind="ExternalInput").ap()
    wg = dt("wg", [D, E], F32, kind="ExternalInput").ap()
    w1 = dt("w1", [E, D, FF], F32, kind="ExternalInput").ap()
    w2 = dt("w2", [E, FF, D], F32, kind="ExternalInput").ap()
    masks = dt("masks", [5, 128, 512], BF16, kind="ExternalInput").ap()
    eye_b = dt("eye_b", [128, 128], BF16, kind="ExternalInput").ap()
    eye_f = dt("eye_f", [128, 128], F32, kind="ExternalInput").ap()
    ones_cb = dt("ones_cb", [128, 1], BF16, kind="ExternalInput").ap()
    ones_rb = dt("ones_rb", [1, 128], BF16, kind="ExternalInput").ap()
    ones_rf = dt("ones_rf", [1, 128], F32, kind="ExternalInput").ap()
    hm = dt("hm", [64, 6], F32, kind="ExternalInput").ap()
    selown = dt("selown", [64, 8], F32, kind="ExternalInput").ap()
    pat864 = dt("pat864", [8, 64], F32, kind="ExternalInput").ap()
    grp648 = dt("grp648", [64, 8], F32, kind="ExternalInput").ap()
    grp864 = dt("grp864", [8, 64], F32, kind="ExternalInput").ap()
    eyee = dt("eyee", [8, 8], F32, kind="ExternalInput").ap()
    tile648 = dt("tile648", [64, 8], F32, kind="ExternalInput").ap()
    esel8 = dt("esel8", [8, 1024], BF16, kind="ExternalInput").ap()
    y = dt("y", [TOK, D], F32, kind="ExternalOutput").ap()
    if DEBUG_DUMPS:
        dbg_pem = dt("dbg_pem", [8, TOK], F32, kind="ExternalOutput").ap()
        dbg_mpr = dt("dbg_mpr", [8, TOK], F32, kind="ExternalOutput").ap()
        dbg_rmx = dt("dbg_rmx", [1, TOK], F32, kind="ExternalOutput").ap()
        dbg_lo = dt("dbg_lo", [8, 1], F32, kind="ExternalOutput").ap()
        dbg_attn = dt("dbg_attn", [128, 8 * TOK], F32, kind="ExternalOutput").ap()
        dbg_oall = dt("dbg_oall", [128, 8 * TOK], BF16, kind="ExternalOutput").ap()
        dbg_xn = dt("dbg_xn", [128, 8 * PW], BF16, kind="ExternalOutput").ap()
        dbg_qphi = dt("dbg_qphi", [64, 2054], BF16, kind="ExternalOutput").ap()
        dbg_st = dt("dbg_st", [128, 144], F32, kind="ExternalOutput").ap()
        dbg_kphi = dt("dbg_kphi", [64, 2054], BF16, kind="ExternalOutput").ap()

    cc_in = dt("cc_in", [8, TOK], F32, kind="Internal").ap()
    cc_out = dt("cc_out", [64, TOK], F32, kind="Internal").ap()

    with ExitStack() as ctx:
        bfa = ctx.enter_context(nc.sbuf_tensor("bfa", [128, NBF], BF16))
        fpa = ctx.enter_context(nc.sbuf_tensor("fpa", [128, NFP], F32))
        pb = [ctx.enter_context(nc.psum_tensor(f"pb{i}", [128, 512], F32))
              for i in range(8)]
        YP = [SemC(ctx.enter_context(nc.semaphore(f"Y{i}"))) for i in range(4)]
        GP = [SemC(ctx.enter_context(nc.semaphore(f"Gq{i}"))) for i in range(4)]
        STP = [SemC(ctx.enter_context(nc.semaphore(f"SQ{i}"))) for i in range(2)]
        ydma_i = [0]
        gdma_i = [0]
        stdma_i = [0]
        CC = SemC(ctx.enter_context(nc.semaphore("CC")))
        T = SemC(ctx.enter_context(nc.semaphore("T")))
        V = SemC(ctx.enter_context(nc.semaphore("V")))
        S = SemC(ctx.enter_context(nc.semaphore("S")))
        G = SemC(ctx.enter_context(nc.semaphore("G")))
        block = ctx.enter_context(nc.Block())

        streams = {"t": [], "v": [], "s": [], "g": [], "y": []}

        def em(eng, f):
            streams[eng].append(f)

        def barrier():
            if not SERIAL:
                return
            sems = [T, V, S, G] + YP + GP + STP + [CC]
            for eng in ("t", "v", "s", "g", "y"):
                for sm in sems:
                    if sm.n > 0:
                        wt(eng, sm, sm.n)

        def wt(eng, sem, val=None):
            if val is None:
                sem, val = sem
            if val > 0:
                em(eng, lambda e, h=sem.h, v=val: e.wait_ge(h, v))

        def vsync():
            em("v", lambda e, v=V.n: e.wait_ge(V.h, v))

        def wtall(eng, snap):
            for sem, val in snap:
                wt(eng, sem, val)

        def bfv(col, w, p0=0, p1=128):
            return bfa[p0:p1, col:col + w]

        def fpv(col, w, p0=0, p1=128):
            return fpa[p0:p1, col:col + w]

        def _pool_dma(stream, pool, idx, src, dst_view, wait):
            sem = pool[idx[0] % len(pool)]
            idx[0] += 1
            if wait is not None:
                wt(stream, wait)
            if sem.n > 0:
                wt(stream, sem, sem.n)
            n = sem.inc(16)
            em(stream, lambda e, s=src, d=dst_view, h=sem.h:
               e.dma_start(out=d, in_=s).then_inc(h, 16))
            return (sem, n)

        def ydma(src, dst_view, wait=None):
            return _pool_dma("y", YP, ydma_i, src, dst_view, wait)

        def gdma(src, dst_view, wait=None):
            return _pool_dma("g", GP, gdma_i, src, dst_view, wait)

        def stdma(src_view, dst, wait=None):
            return _pool_dma("y", STP, stdma_i, src_view, dst, wait)

        ydma(eye_f, fpv(EYEF, 128))
        ydma(ones_rf, fpa[0:1, ONES1F:ONES1F + 128])
        ydma(boutT, fpv(BOUTT, 8))
        ydma(hm, fpa[0:64, HM:HM + 6])
        ydma(selown, fpa[0:64, SELOWN:SELOWN + 8])
        ydma(pat864, fpa[0:8, PAT864:PAT864 + 64])
        ydma(grp648, fpa[0:64, GRP648:GRP648 + 8])
        ydma(grp864, fpa[0:8, GRP864:GRP864 + 64])
        ydma(eyee, fpa[0:8, EYEE:EYEE + 8])
        ydma(tile648, fpa[0:64, TILE648:TILE648 + 8])
        ydma(esel8, bfa[0:8, ESEL8:ESEL8 + 1024])
        ydma(masks.rearrange("m p f -> p m f"),
             bfv(MASK0, 2560).rearrange("p (m f) -> p m f", m=5))
        ydma(eye_b, bfv(EYEB, 128))
        ydma(ones_cb, bfv(ONES128B, 1))
        ydma(ones_rb, bfa[0:1, ONES1B:ONES1B + 128])
        X_consts = [(s, s.n) for s in YP if s.n > 0]

        W_proj = gdma(proj, bfv(PROJB, 64))
        W_wg = gdma(wg.rearrange("(c p) e -> p c e", p=128),
                    bfv(SWG, 64).rearrange("p (c e) -> p c e", c=8))
        def whead_dma(h, wait_t):
            toks = []
            for i, base in enumerate((0, D, 2 * D)):
                toks.append(gdma(
                    wqkv[:, base + h * DH:base + (h + 1) * DH]
                    .rearrange("(c p) f -> p c f", p=128),
                    bfv(WHEAD + i * 1024, 1024).rearrange("p (c f) -> p c f", c=8),
                    wait=(T, wait_t) if i == 0 and wait_t else None))
            return toks
        W_whead = [None] * (H + 1)
        W_whead[0] = whead_dma(0, 0)
        def gmemset(view):
            G.inc(1)
            em("g", lambda e, d=view: e.memset(d, 0.0).then_inc(G.h, 1))
        gmemset(bfa[0:64, KPHI + 2056:KPHI + PW])
        for c in range(8):
            gmemset(bfa[:, XN[c]:XN[c] + 1])
            gmemset(bfa[:, XN[c] + 2055:XN[c] + PW])
        G_zeros = G.n

        SWOUT_LO = RA
        SWOUT_HI = RA + 4352
        SWOUT_COL = [SWOUT_LO + kc * 1024 if kc < 4 else SWOUT_HI + (kc - 4) * 1024
                     for kc in range(8)]
        W_swout = [None, None]

        V_xn = [0] * 18
        S_trev = [[0] * 8 for _ in range(18)]
        T_tr = [[0] * 8 for _ in range(18)]
        X_xt = [0] * 18

        def tile_rows(t):
            return 3 if t >= 16 else 128

        def tile_dst_col(t):
            if t < 16:
                return OW0 + 128 * t
            return 1 if t == 16 else 2052

        V.inc()
        em("v", lambda e, o=fpv(EPSC, 1): e.memset(o, 1e-5).then_inc(V.h, 1))
        for t in range(18):
            src = xm[t * 128:(t + 1) * 128, :] if t < 16 else (xhl if t == 16 else xhr)
            r = tile_rows(t)
            X_xt[t] = ydma(src, fpv(XT[t % 2], 1024, 0, r),
                           wait=(V, V_xn[t - 2]) if t >= 2 else None)
            xt = fpv(XT[t % 2], 1024, 0, r)
            stc = STATS + 8 * t
            wt("s", X_xt[t])
            S.inc()
            em("s", lambda e, xv=xt, o=fpv(SQD, 1024, 0, r), a=fpv(stc + 1, 1, 0, r):
               e.activation(o, xv, AF.Square, accum_out=a).then_inc(S.h, 1))
            S.inc()
            em("s", lambda e, xv=xt, o=fpv(SQD, 1024, 0, r), a=fpv(stc + 0, 1, 0, r):
               e.activation(o, xv, AF.Copy, accum_out=a).then_inc(S.h, 1))
            S_sq = S.n
            wt("v", S, S_sq)
            for fdef in (
                lambda e, i=fpv(stc, 1, 0, r), o=fpv(stc + 2, 1, 0, r):
                    e.tensor_scalar_mul(o, i, -1.0 / D),
                lambda e, a=fpv(stc, 1, 0, r), b=fpv(stc + 2, 1, 0, r),
                       o=fpv(stc + 3, 1, 0, r):
                    e.tensor_mul(o, a, b),
                lambda e, i=fpv(stc + 1, 1, 0, r), o=fpv(stc + 4, 1, 0, r):
                    e.tensor_scalar_mul(o, i, 1.0 / D),
                lambda e, a=fpv(stc + 3, 1, 0, r), b=fpv(stc + 4, 1, 0, r),
                       o=fpv(stc + 5, 1, 0, r):
                    e.scalar_tensor_tensor(o, a, 1.0 / D, b, ALU.mult, ALU.add),
            ):
                vsync()
                V.inc()
                em("v", lambda e, f=fdef: f(e).then_inc(V.h, 1))
            V_var = V.n
            wt("s", V, V_var)
            S.inc()
            em("s", lambda e, i=fpv(stc + 5, 1, 0, r), o=fpv(stc + 6, 1, 0, r),
                   b=fpv(EPSC, 1, 0, r):
               e.activation(o, i, AF.Sqrt, bias=b).then_inc(S.h, 1))
            S_rs = S.n
            wt("v", S, S_rs)
            V.inc()
            em("v", lambda e, i=fpv(stc + 6, 1, 0, r), o=fpv(stc + 7, 1, 0, r):
               e.reciprocal(o, i).then_inc(V.h, 1))
            vsync()
            xtm_col = SST + (t % 2) * 1024
            if t >= 2:
                wt("v", T, T_tr[t - 2][7])
            V.inc()
            em("v", lambda e, xv=xt, m=fpv(stc + 2, 1, 0, r), sdv=fpv(stc + 7, 1, 0, r),
                   o=bfv(xtm_col, 1024, 0, r):
               e.tensor_scalar(o, xv, m, sdv, ALU.add, ALU.mult).then_inc(V.h, 1))
            V_xn[t] = V.n
            dst = tile_dst_col(t)
            for c in range(8):
                pbank = pb[6 + ((t * 8 + c) % 2)]
                prev = (t * 8 + c) - 2
                if prev >= 0:
                    wt("t", S, S_trev[prev // 8][prev % 8])
                if c == 0:
                    wt("t", V, V_xn[t])
                    if t == 0:
                            wtall("t", X_consts)
                T.inc()
                if r == 128:
                    em("t", lambda e, l=bfv(xtm_col + c * 128, 128), rh=bfv(EYEB, 128),
                           o=pbank[:, 0:128]:
                       e.matmul(o, l, rh, start=True, stop=True).then_inc(T.h, 1))
                else:
                    em("t", lambda e, l=bfv(xtm_col + c * 128, 128, 0, 3),
                           rh=bfv(EYEB, 3, 0, 3), o=pbank[:, 0:3]:
                       e.matmul(o, l, rh, start=True, stop=True).then_inc(T.h, 1))
                T_tr[t][c] = T.n
                wt("s", T, T_tr[t][c])
                wd = 3 if r == 3 else 128
                S.inc()
                em("s", lambda e, i=pbank[:, 0:wd], o=bfv(XN[c] + dst, wd), wd=wd:
                   e.copy(o, i).then_inc(S.h, 1))
                S_trev[t][c] = S.n
        S_ln_done = S.n
        barrier()
        S_xn_main = S_trev[15][7]

        V_lgt = [0] * 4
        for ts in range(4):
            wt("t", S, S_ln_done)
            if ts == 0:
                wt("t", W_wg)
                wt("t", W_proj)
            if ts > 0:
                wt("t", V, V_lgt[ts - 1])
            for kc in range(8):
                T.inc()
                em("t", lambda e, l=bfv(SWG + kc * 8, 8), rh=bfv(XN[kc] + OW0 + ts * 512, 512),
                       o=pb[7][0:8, :], st=(kc == 0), sp=(kc == 7):
                   e.matmul(o, l, rh, start=st, stop=sp).then_inc(T.h, 1))
            wt("v", T, T.n)
            V.inc()
            em("v", lambda e, i=pb[7][0:8, :], o=fpv(LGT + ts * 512, 512, 0, 8):
               e.tensor_copy(o, i).then_inc(V.h, 1))
            V_lgt[ts] = V.n
        wt("y", V, V_lgt[3])
        CC.inc(16)
        em("y", lambda e: e.dma_start(out=cc_in, in_=fpv(LGT, 2048, 0, 8)).then_inc(CC.h, 16))
        wt("g", CC, 16)
        CC.inc(1)
        em("g", lambda e: e.collective_compute(
            "AllGather", ALU.bypass, replica_groups=[list(range(NC))],
            ins=[cc_in], outs=[cc_out]).then_inc(CC.h, 1))
        wt("y", CC, 17)
        CC.inc(16)
        em("y", lambda e: e.dma_start(out=fpv(AGC, 2048, 0, 64), in_=cc_out).then_inc(CC.h, 16))
        wt("s", CC, 33)
        S.inc()
        em("s", lambda e, v=fpv(AGC, 2048, 0, 64): e.activation(v, v, AF.Exp).then_inc(S.h, 1))
        S_exp = S.n
        V_sm = [0] * 4
        for ts in range(4):
            wt("t", S, S_exp)
            if ts > 0:
                wt("t", V, V_sm[ts - 1])
            T.inc()
            em("t", lambda e, l=fpa[0:64, GRP648:GRP648 + 8],
                   rh=fpv(AGC + ts * 512, 512, 0, 64), o=pb[7][0:8, :]:
               e.matmul(o, l, rh, start=True, stop=True).then_inc(T.h, 1))
            wt("v", T, T.n)
            V.inc()
            em("v", lambda e, i=pb[7][0:8, :], o=fpv(D8S, 512, 0, 8):
               e.reciprocal(o, i).then_inc(V.h, 1))
            Vr = V.n
            wt("t", V, Vr)
            T.inc()
            em("t", lambda e, l=fpa[0:8, GRP864:GRP864 + 64], rh=fpv(D8S, 512, 0, 8),
                   o=pb[7][0:64, :]:
               e.matmul(o, l, rh, start=True, stop=True).then_inc(T.h, 1))
            wt("v", T, T.n)
            V.inc()
            em("v", lambda e, p=fpv(AGC + ts * 512, 512, 0, 64), i=pb[7][0:64, :]:
               e.tensor_tensor(p, p, i, ALU.mult).then_inc(V.h, 1))
            V_sm[ts] = V.n
        V_probs = V.n
        barrier()
        for col, val in ((LO8, 0.0), (HI8, 1.0)):
            V.inc()
            em("v", lambda e, o=fpv(col, 1, 0, 8), vv=val: e.memset(o, vv).then_inc(V.h, 1))
        V_bis_prev = [V.n]
        T_bis_prev = [0]
        DO_HEADS = STOP_AFTER not in ("ln", "route")
        H_CUT = H
        if STOP_AFTER and STOP_AFTER.startswith("h"):
            H_CUT = int(STOP_AFTER[1:])
        DO_OUTPROJ = STOP_AFTER in (None, "attn", "bis") and DO_HEADS
        DO_BISECT = STOP_AFTER in (None, "bis")
        DO_FFN = STOP_AFTER is None or (STOP_AFTER or "").startswith("ffn")
        FFN_GR = 16
        FFN_LVL = 9
        if STOP_AFTER and STOP_AFTER.startswith("ffn"):
            parts = STOP_AFTER[3:].split("_")
            FFN_GR = int(parts[0])
            if len(parts) > 1:
                FFN_LVL = int(parts[1])

        bis_state = {"k": 0, "counted": 0}

        def emit_bisect_theta():
            V.inc()
            em("v", lambda e, o=fpv(TH8, 1, 0, 8), a=fpv(LO8, 1, 0, 8), b=fpv(HI8, 1, 0, 8):
               e.tensor_add(o, a, b).then_inc(V.h, 1))
            vsync()
            V.inc()
            em("v", lambda e, o=fpv(TH8, 1, 0, 8):
               e.tensor_scalar_mul(o, o, 0.5).then_inc(V.h, 1))
            bis_state["thV"] = V.n

        def emit_bisect_phaseA():
            if "thV" not in bis_state:
                emit_bisect_theta()
            Vth = bis_state.pop("thV")
            if "cntS" in bis_state:
                wt("t", S, bis_state["cntS"])
            wt("t", V, Vth)
            T.inc()
            em("t", lambda e, l=fpa[0:8, PAT864:PAT864 + 64], rh=fpv(TH8, 1, 0, 8),
                   o=pb[7][0:64, 0:1]:
               e.matmul(o, l, rh, start=True, stop=True).then_inc(T.h, 1))
            T_bis_prev[0] = T.n
            wt("s", T, T.n)
            S.inc()
            em("s", lambda e, i=pb[7][0:64, 0:1], o=fpv(TH64, 1, 0, 64):
               e.copy(o, i).then_inc(S.h, 1))
            Sc = S.n
            wt("v", S, Sc)
            V.inc()
            em("v", lambda e, i=fpv(AGC, 2048, 0, 64), th=fpv(TH64, 1, 0, 64),
                   o=bfv(BFDUM, 2048, 0, 64), a=fpv(CNT64, 1, 0, 64):
               e.tensor_scalar(o, i, th, 0.0, ALU.is_gt, ALU.add,
                               accum_out=a).then_inc(V.h, 1))
            bis_state["cmpV"] = V.n

        def emit_bisect_phaseB():
            wt("t", V, bis_state["cmpV"])
            T.inc()
            em("t", lambda e, l=fpa[0:64, TILE648:TILE648 + 8], rh=fpv(CNT64, 1, 0, 64),
                   o=pb[7][0:8, 4:5]:
               e.matmul(o, l, rh, start=True, stop=True).then_inc(T.h, 1))
            wt("s", T, T.n)
            S.inc()
            em("s", lambda e, i=pb[7][0:8, 4:5], o=fpv(CNT8, 1, 0, 8):
               e.copy(o, i).then_inc(S.h, 1))
            Sc2 = S.n
            bis_state["cntS"] = S.n
            wt("v", S, Sc2)
            for fdef in (
                lambda e: e.tensor_scalar(fpv(ISGE, 1, 0, 8), fpv(CNT8, 1, 0, 8),
                                          CAP - 0.5, None, ALU.is_gt),
                lambda e: e.tensor_sub(fpv(TMP8, 1, 0, 8), fpv(TH8, 1, 0, 8), fpv(LO8, 1, 0, 8)),
                lambda e: e.tensor_mul(fpv(TMP8, 1, 0, 8), fpv(TMP8, 1, 0, 8), fpv(ISGE, 1, 0, 8)),
                lambda e: e.tensor_add(fpv(LO8, 1, 0, 8), fpv(LO8, 1, 0, 8), fpv(TMP8, 1, 0, 8)),
                lambda e: e.tensor_scalar(fpv(ISLT, 1, 0, 8), fpv(ISGE, 1, 0, 8), -1.0, -1.0,
                                          ALU.add, ALU.mult),
                lambda e: e.tensor_sub(fpv(TMP8, 1, 0, 8), fpv(TH8, 1, 0, 8), fpv(HI8, 1, 0, 8)),
                lambda e: e.tensor_mul(fpv(TMP8, 1, 0, 8), fpv(TMP8, 1, 0, 8), fpv(ISLT, 1, 0, 8)),
                lambda e: e.tensor_add(fpv(HI8, 1, 0, 8), fpv(HI8, 1, 0, 8), fpv(TMP8, 1, 0, 8)),
            ):
                vsync()
                V.inc()
                em("v", lambda e, f=fdef: f(e).then_inc(V.h, 1))
            V_bis_prev[0] = V.n
            wt("v", T, T_bis_prev[0])
            emit_bisect_theta()

        KOFF = [0, 512, 1024, 1536, 2048]
        KWID = [512, 512, 512, 512, 8]
        bis_emitted = [0]
        prev_szb = [0]
        pend = [None]

        def flush_oevac():
            if pend[0] is None:
                return
            hh, gg, Vz, TO, pO = pend[0]
            pend[0] = None
            wt("t", V, Vz)
            if prev_szb[0]:
                wt("t", S, prev_szb[0])
            T.inc()
            em("t", lambda e, l=fpa[0:1, ONES1F:ONES1F + 128], rh=fpv(ZSC, 512, 0, 1),
                   o=pb[4][:, :]:
               e.matmul(o, l, rh, start=True, stop=True).then_inc(T.h, 1))
            Tzb = T.n
            wt("s", T, Tzb)
            if V_oev_last[0]:
                wt("s", V, V_oev_last[0])
            S.inc()
            em("s", lambda e, i=pb[4][:, :], o=fpv(ZBS, 512):
               e.copy(o, i).then_inc(S.h, 1))
            Szb = S.n
            prev_szb[0] = Szb
            wt("v", T, TO)
            wt("v", S, Szb)
            V.inc()
            em("v", lambda e, i=pO[:, :], z=fpv(ZBS, 512),
                   o=bfv(OALL + hh * 2048 + gg * 512, 512):
               e.tensor_tensor(o, i, z, ALU.mult).then_inc(V.h, 1))
            V_oev_prev[(hh, gg)] = V.n
            V_oev_last[0] = V.n

        V_oev_last = [0]
        S_vtm = [[0] * 17 for _ in range(H)]
        V_oall = [0] * H
        V_mask_prev = {}
        V_oev_prev = {}
        T_vlast = [0] * H
        T_head_last = [0] * H

        for h in range(H_CUT if DO_HEADS else 0):
            wtall("t", W_whead[h])
            wtall("t", X_consts)
            if h > 0:
                wt("t", V, V_oall[h - 1])
                if prev_szb[0]:
                    wt("t", S, prev_szb[0])
            S_wqkp = 0
            Tp_prev = 0
            for kc in range(8):
                pt = pb[4 + (kc % 2)]
                po = pb[4 + ((kc + 1) % 2)]
                if S_wqkp:
                    wt("t", S, S_wqkp)
                S_tr = 0
                for qi in range(2):
                    if qi == 1:
                        wt("t", S, S_tr)
                    T.inc()
                    em("t", lambda e, l=bfv(WHEAD + qi * 1024 + kc * 128, 128),
                           rh=bfv(EYEB, 128), o=pt[:, 0:128]:
                       e.matmul(o, l, rh, start=True, stop=True).then_inc(T.h, 1))
                    Ttr = T.n
                    wt("s", T, Ttr)
                    if qi == 0 and Tp_prev:
                        wt("s", T, Tp_prev)
                    S.inc()
                    em("s", lambda e, i=pt[:, 0:128], o=bfv(TRB + qi * 128, 128):
                       e.copy(o, i).then_inc(S.h, 1))
                    S_tr = S.n
                wt("t", S, S_tr)
                for qi in range(2):
                    T.inc()
                    em("t", lambda e, l=bfv(TRB + qi * 128, 128), rh=bfv(PROJB, 64),
                           o=po[:, qi * 64:(qi + 1) * 64]:
                       e.matmul(o, l, rh, start=True, stop=True).then_inc(T.h, 1))
                Tp = T.n
                Tp_prev = Tp
                wt("s", T, Tp)
                S.inc()
                em("s", lambda e, i=po[:, 0:128], o=bfv(WQKP + kc * 128, 128):
                   e.copy(o, i).then_inc(S.h, 1))
                S_wqkp = S.n
            for b in range(0 if HEAD_STAGE == "pre" else 17):
                pbank = pb[b % 2]
                if b >= 2:
                    wt("t", S, S_vtm[h][b - 2])
                elif h > 0 and b < 2:
                    wt("t", S, S_vtm[h - 1][15 + b])
                if b == 0:
                    if h == 0:
                        wt("t", S, S_ln_done)
                        wt("t", G, G_zeros)
                for kc in range(8):
                    T.inc()
                    em("t", lambda e, l=bfv(XN[kc] + b * 128, 128),
                           rh=bfv(WHEAD + 2048 + kc * 128, 128),
                           o=pbank[:, 0:128], st=(kc == 0), sp=(kc == 7):
                       e.matmul(o, l, rh, start=st, stop=sp).then_inc(T.h, 1))
                T_vlast[h] = T.n
                wt("s", T, T.n)
                S.inc()
                em("s", lambda e, i=pbank[:, 0:128], o=bfv(VTM + b * 128, 128):
                   e.copy(o, i).then_inc(S.h, 1))
                S_vtm[h][b] = S.n
            if h + 1 < H:
                W_whead[h + 1] = whead_dma(h + 1, T_vlast[h])
            elif h == H - 1:
                pass
            S_phi = 0
            V_phi = 0
            qjobs = [(0, QPHI, OW0 + 512 * g, 512) for g in range(4)]
            kjobs = [(64, KPHI, o, 512) for o in (0, 512, 1024, 1536, 1664)]
            phi_jobs = [] if HEAD_STAGE in ("pre", "v") else qjobs + kjobs
            V_phi_list = [0] * len(phi_jobs)
            V_min_list = [0] * len(phi_jobs)
            S_relu_list = [0] * len(phi_jobs)
            for ji, (wcol, dstc, off, wd) in enumerate(phi_jobs):
                pbank = pb[2 + (ji % 2)]
                wt("t", S, S_wqkp)
                if ji >= 2:
                    wt("t", V, V_min_list[ji - 2])
                for kc in range(8):
                    T.inc()
                    em("t", lambda e, l=bfv(WQKP + kc * 128 + wcol, 64),
                           rh=bfv(XN[kc] + off, wd), o=pbank[0:64, 0:wd],
                           st=(kc == 0), sp=(kc == 7):
                       e.matmul(o, l, rh, start=st, stop=sp).then_inc(T.h, 1))
                Tph = T.n
                wt("v", T, Tph)
                V.inc()
                em("v", lambda e, i=pbank[0:64, 0:wd], o=fpv(SCA, wd, 0, 64):
                   e.tensor_scalar_max(o, i, 0.0).then_inc(V.h, 1))
                V.inc()
                em("v", lambda e, i=pbank[0:64, 0:wd], o=fpv(SCB, wd, 0, 64):
                   e.tensor_scalar_min(o, i, 0.0).then_inc(V.h, 1))
                V_min_list[ji] = V.n
                wt("s", V, V.n)
                S.inc()
                em("s", lambda e, o=fpv(SCB, wd, 0, 64):
                   e.activation(o, o, AF.Exp).then_inc(S.h, 1))
                S_phi = S.n
                wt("v", S, S_phi)
                V.inc()
                em("v", lambda e, a=fpv(SCA, wd, 0, 64), b=fpv(SCB, wd, 0, 64),
                       o=bfv(dstc + off, wd, 0, 64):
                   e.tensor_add(o, a, b).then_inc(V.h, 1))
                V_phi_list[ji] = V.n
                V_phi = V.n
            T_phi_last = T.n
            for (c0, w0, hmc) in (((1, 3, HM), (2052, 3, HM + 3))
                                  if HEAD_STAGE in (None, "phi") else ()):
                V.inc()
                em("v", lambda e, kv=bfv(KPHI + c0, w0, 0, 64),
                       m=fpv(hmc, w0, 0, 64):
                   e.tensor_tensor(kv, kv, m, ALU.mult).then_inc(V.h, 1))
            V_kphi = V.n
            for g in range(0 if HEAD_STAGE in ("pre", "v", "phi") else 4):
                Ts_tile = [0] * 5
                V_mask = [0] * 5
                for m in range(5):
                    pbank = pb[m % 3]
                    key = (h, g, m)
                    if m >= 3:
                        pkey = (h, g, m - 3)
                    elif g > 0:
                        pkey = (h, g - 1, m + 3 if m < 2 else 2)
                    else:
                        pkey = (h - 1, 3, m + 3 if m < 2 else 2)
                    if pkey in V_mask_prev:
                        wt("t", V, V_mask_prev[pkey])
                    if m == 0 and g == 0:
                        wt("t", V, V_kphi)
                        wt("t", S, S_vtm[h][16])
                    T.inc()
                    em("t", lambda e, l=bfv(KPHI + g * 512 + m * 128, 128, 0, 64),
                           rh=bfv(QPHI + OW0 + g * 512, 512, 0, 64), o=pbank[:, :]:
                       e.matmul(o, l, rh, start=True, stop=True).then_inc(T.h, 1))
                    Ts_tile[m] = T.n
                    wt("v", T, Ts_tile[m])
                    V.inc()
                    em("v", lambda e, i=pbank[:, :], mk=bfv(MASK0 + m * 512, 512),
                           o=bfv(SST + m * 512, 512):
                       e.tensor_tensor(o, i, mk, ALU.mult).then_inc(V.h, 1))
                    V_mask[m] = V.n
                    V_mask_prev[key] = V.n
                if DO_BISECT and bis_state["counted"] < bis_state["k"]:
                    emit_bisect_phaseB()
                    bis_state["counted"] += 1
                flush_oevac()
                wt("t", V, V_mask[4])
                for m in range(5):
                    T.inc()
                    em("t", lambda e, l=bfv(ONES128B, 1), rh=bfv(SST + m * 512, 512),
                           o=pb[3][0:1, :], st=(m == 0), sp=(m == 4):
                       e.matmul(o, l, rh, start=st, stop=sp).then_inc(T.h, 1))
                Tz = T.n
                wt("v", T, Tz)
                V.inc()
                em("v", lambda e, i=pb[3][0:1, :], o=fpv(ZSC, 512, 0, 1):
                   e.tensor_scalar_add(o, i, 1e-8).then_inc(V.h, 1))
                vsync()
                V.inc()
                em("v", lambda e, o=fpv(ZSC, 512, 0, 1): e.reciprocal(o, o).then_inc(V.h, 1))
                Vz = V.n
                pO = pb[5 + (g % 2)]
                okey = (h, g - 2) if g >= 2 else (h - 1, g + 2)
                if okey in V_oev_prev:
                    wt("t", V, V_oev_prev[okey])
                if g < 2:
                    wt("t", S, S_wqkp)
                for m in range(5):
                    T.inc()
                    em("t", lambda e, l=bfv(VTM + (g * 4 + m) * 128, 128),
                           rh=bfv(SST + m * 512, 512), o=pO[:, :],
                           st=(m == 0), sp=(m == 4):
                       e.matmul(o, l, rh, start=st, stop=sp).then_inc(T.h, 1))
                TO = T.n
                T_head_last[h] = T.n
                pend[0] = (h, g, Vz, TO, pO)
                if DO_BISECT and bis_state["counted"] < bis_state["k"]:
                    emit_bisect_phaseB()
                    bis_state["counted"] += 1
                if DO_BISECT and bis_state["k"] < BISECT:
                    emit_bisect_phaseA()
                    bis_state["k"] += 1
            flush_oevac()
            V_oall[h] = V.n if HEAD_STAGE is None else max(V.n, 1)
            barrier()
            if h == H - 1 and DO_OUTPROJ:
                for kc in range(4):
                    W_swout[0] = gdma(wout[kc * 128:(kc + 1) * 128, :],
                                      bfv(SWOUT_COL[kc], 1024),
                                      wait=(T, T_phi_last) if kc == 0 else None)
                for kc in range(4, 8):
                    W_swout[1] = gdma(wout[kc * 128:(kc + 1) * 128, :],
                                      bfv(SWOUT_COL[kc], 1024),
                                      wait=(T, T_head_last[h]) if kc == 4 else None)


        if DO_BISECT and bis_state["counted"] < bis_state["k"]:
            emit_bisect_phaseB()
            bis_state["counted"] += 1
        V_pem = [0] * 4
        for ts in range(4 if DO_BISECT else 0):
            wt("t", V, V_bis_prev[0])
            if ts > 0:
                wt("t", V, V_pem[ts - 1])
            T.inc()
            em("t", lambda e, l=fpa[0:64, SELOWN:SELOWN + 8],
                   rh=fpv(AGC + ts * 512, 512, 0, 64), o=pb[7][0:8, :]:
               e.matmul(o, l, rh, start=True, stop=True).then_inc(T.h, 1))
            wt("v", T, T.n)
            V.inc()
            em("v", lambda e, i=pb[7][0:8, :], o=fpv(PEM + ts * 512, 512, 0, 8):
               e.tensor_copy(o, i).then_inc(V.h, 1))
            V_pem[ts] = V.n
        if DO_BISECT:
            V.inc()
            em("v", lambda e, i=fpv(PEM, 2048, 0, 8), th=fpv(LO8, 1, 0, 8),
                   o=fpv(SCR, 2048, 0, 8):
               e.tensor_scalar(o, i, th, None, ALU.is_gt).then_inc(V.h, 1))
            V.inc()
            em("v", lambda e, s=fpv(SCR, 2048, 0, 8), p=fpv(PEM, 2048, 0, 8),
                   o=fpv(MPR, 2048, 0, 8):
               e.scalar_tensor_tensor(o, s, 2.0, p, ALU.mult, ALU.add).then_inc(V.h, 1))
        V_mpr = V.n
        V_rmx_done = 0
        for ts in range(4 if DO_BISECT else 0):
            for ee in range(8):
                pbank = pb[6 + (ee % 2)]
                wt("t", V, V_mpr if ee == 0 and ts == 0 else V.n)
                T.inc()
                em("t", lambda e, l=fpa[0:8, EYEE + ee:EYEE + ee + 1],
                       rh=fpv(MPR + ts * 512, 512, 0, 8), o=pbank[0:1, :]:
                   e.matmul(o, l, rh, start=True, stop=True).then_inc(T.h, 1))
                Tx = T.n
                wt("v", T, Tx)
                V.inc()
                if ee == 0:
                    em("v", lambda e, i=pbank[0:1, :], o=fpv(RMX + ts * 512, 512, 0, 1):
                       e.tensor_copy(o, i).then_inc(V.h, 1))
                else:
                    em("v", lambda e, i=pbank[0:1, :], o=fpv(RMX + ts * 512, 512, 0, 1):
                       e.tensor_tensor(o, o, i, ALU.max).then_inc(V.h, 1))
        V_rmx_done = V.n

        V_opj = {}
        V_opj_last = V.n
        T_opj_last = T.n
        for ts in range(4 if DO_OUTPROJ else 0):
            for c in range(8):
                pbank = pb[(ts * 8 + c) % 4]
                pkey = (ts * 8 + c) - 4
                if pkey >= 0:
                    wt("t", V, V_opj[pkey])
                if ts == 0 and c == 0:
                    wt("t", V, V_oall[H - 1])
                    wt("t", W_swout[0])
                for kc in range(8):
                    if ts == 0 and c == 0 and kc == 4:
                        wt("t", W_swout[1])
                    T.inc()
                    em("t", lambda e, l=bfv(SWOUT_COL[kc] + c * 128, 128),
                           rh=bfv(OALL + kc * 2048 + ts * 512, 512), o=pbank[:, :],
                           st=(kc == 0), sp=(kc == 7):
                       e.matmul(o, l, rh, start=st, stop=sp).then_inc(T.h, 1))
                Tp = T.n
                wt("v", T, Tp)
                V.inc()
                em("v", lambda e, i=pbank[:, :], b=fpv(BOUTT + c, 1),
                       o=fpv(OACC[c] + ts * 512, 512):
                   e.tensor_scalar(o, i, b, None, ALU.add).then_inc(V.h, 1))
                V_opj[ts * 8 + c] = V.n
        if DO_HEADS:
            T_opj_last = T.n
            V_opj_last = V.n
        dbg_toks = []
        if DEBUG_DUMPS:
            dbg_toks.append(ydma(fpv(BIG, 8 * TOK), dbg_attn, wait=(V, V_opj_last)))
            dbg_toks.append(ydma(bfv(OALL, 8 * TOK), dbg_oall))
            dbg_toks.append(ydma(bfv(0, 8 * PW), dbg_xn))
            dbg_toks.append(ydma(fpv(PEM, TOK, 0, 8), dbg_pem))
            dbg_toks.append(ydma(fpv(MPR, TOK, 0, 8), dbg_mpr))
            dbg_toks.append(ydma(fpv(RMX, TOK, 0, 1), dbg_rmx))
            dbg_toks.append(ydma(fpv(LO8, 1, 0, 8), dbg_lo))
            dbg_toks.append(ydma(bfv(QPHI + 1, 2054, 0, 64), dbg_qphi))
            dbg_toks.append(ydma(fpv(STATS, 144), dbg_st))
            dbg_toks.append(ydma(bfv(KPHI + 1, 2054, 0, 64), dbg_kphi))
            for tok in dbg_toks:
                wt("v", tok)
                wt("s", tok)

        def w1_dma(e_, fh, wait_t):
            return gdma(w1[e_, :, fh * 1024:(fh + 1) * 1024]
                        .rearrange("(kc p) f -> p kc f", p=128),
                        bfv(W1B, 8192).rearrange("p (kc f) -> p kc f", kc=8),
                        wait=(T, wait_t) if wait_t else None)

        def w2_dma(e_, fh, wait_t):
            return gdma(w2[e_, fh * 1024:(fh + 1) * 1024, :]
                        .rearrange("(kc p) f -> p kc f", p=128),
                        bfv(W2B, 8192).rearrange("p (kc f) -> p kc f", kc=8),
                        wait=(T, wait_t) if wait_t else None)

        granules = ([(e_, fh) for e_ in range(E) for fh in range(2)][:FFN_GR]
                    if DO_FFN else [])
        W_w1 = {}
        W_w2 = {}
        if DEBUG_DUMPS:
            for tok in dbg_toks:
                wt("g", tok)
        if DO_FFN:
            W_w1[granules[0]] = w1_dma(0, 0, T_opj_last)
            W_w2[granules[0]] = w2_dma(0, 0, T_opj_last)

        V_gem = 0
        if DO_BISECT:
            for ts in range(4):
                wt("t", V, V_rmx_done if ts == 0 else V_gem)
                T.inc()
                em("t", lambda e, l=fpa[0:1, ONES1F:ONES1F + 8],
                       rh=fpv(RMX + ts * 512, 512, 0, 1), o=pb[7][0:8, :]:
                   e.matmul(o, l, rh, start=True, stop=True).then_inc(T.h, 1))
                wt("s", T, T.n)
                S.inc()
                em("s", lambda e, i=pb[7][0:8, :], o=fpv(GSC, 512, 0, 8):
                   e.copy(o, i).then_inc(S.h, 1))
                wt("v", S, S.n)
                V.inc()
                em("v", lambda e, a=fpv(GSC, 512, 0, 8), m=fpv(MPR + ts * 512, 512, 0, 8):
                   e.tensor_tensor(a, m, a, ALU.is_ge).then_inc(V.h, 1))
                V.inc()
                em("v", lambda e, a=fpv(GSC, 512, 0, 8), p=fpv(PEM + ts * 512, 512, 0, 8),
                       o=bfv(BFDUM + ts * 512, 512, 0, 8):
                   e.tensor_tensor(o, a, p, ALU.mult).then_inc(V.h, 1))
                V_gem = V.n

        V_gbe = [0] * E
        v_gp_prev = [V_gem]

        def emit_gate_prep(e_):
            v_prev = v_gp_prev
            for ts in range(4):
                wt("t", V, max(V_gem, v_prev[0]))
                pbank = pb[6 + (ts % 2)]
                T.inc()
                em("t", lambda e, l=bfa[0:8, ESEL8 + e_ * 128:ESEL8 + (e_ + 1) * 128],
                       rh=bfv(BFDUM + ts * 512, 512, 0, 8), o=pbank[:, :]:
                   e.matmul(o, l, rh, start=True, stop=True).then_inc(T.h, 1))
                Tb = T.n
                wt("v", T, Tb)
                V.inc()
                em("v", lambda e, i=pbank[:, :], o=bfv(GBE + ts * 512, 512):
                   e.tensor_copy(o, i).then_inc(V.h, 1))
                v_prev[0] = V.n
            V_gbe[e_] = V.n

        T_u_last = 0
        T_f2_last = T_opj_last
        V_f2 = {}
        S_gelu = {}
        V_gate = {}
        for gi, (e_, fh) in enumerate(granules):
            if fh == 0 and FFN_LVL >= 3:
                emit_gate_prep(e_)
            for th in range(2):
                for ts in range(2):
                    for f in range(8):
                        pbank = pb[f % 2]
                        idx = (gi, th, ts, f)
                        pidx = (gi, th, ts, f - 2)
                        if f >= 2 and (gi, th, ts, f - 2) in S_gelu:
                            wt("t", S, S_gelu[(gi, th, ts, f - 2)])
                        elif ts == 0 and f < 2:
                            pk = (gi, th - 1, 1, 6 + f)
                            if pk in S_gelu:
                                wt("t", S, S_gelu[pk])
                            pk2 = (gi - 1, 1, 1, 6 + f)
                            if th == 0 and pk2 in S_gelu:
                                wt("t", S, S_gelu[pk2])
                        elif ts == 1 and f < 2:
                            pk = (gi, th, 0, 6 + f)
                            if pk in S_gelu:
                                wt("t", S, S_gelu[pk])
                        if th == 0 and ts == 0 and f == 0:
                            wt("t", W_w1[(e_, fh)])
                            if gi == 0:
                                wt("t", V, V_opj_last)
                        for kc in range(8):
                            T.inc()
                            em("t", lambda e, l=bfv(W1B + kc * 1024 + f * 128, 128),
                                   rh=bfv(XN[kc] + OW0 + th * 1024 + ts * 512, 512),
                                   o=pbank[:, :], st=(kc == 0), sp=(kc == 7):
                               e.matmul(o, l, rh, start=st, stop=sp).then_inc(T.h, 1))
                        T_u_last = T.n
                        Tu = T.n
                        if FFN_LVL < 2:
                            S_gelu[idx] = 0
                            V_gate[(gi, th, ts, f)] = 0
                            continue
                        if ts == 0 and f == 0:
                            wt("s", T, T_f2_last)
                        wt("s", T, Tu)
                        pkg = (gi, th - 1, ts, f) if th == 1 else (gi - 1, 1, ts, f)
                        if pkg in V_gate:
                            wt("s", V, V_gate[pkg])
                        S.inc()
                        em("s", lambda e, i=pbank[:, :], o=fpv(GSC, 512):
                           e.copy(o, i).then_inc(S.h, 1))
                        S_gelu[idx] = S.n
                        S.inc()
                        em("s", lambda e, i=fpv(GSC, 512),
                               o=bfv(ASB + f * 1024 + ts * 512, 512):
                           e.activation(o, i, AF.Gelu).then_inc(S.h, 1))
                        S_gelu2 = S.n
                        if FFN_LVL < 4:
                            V_gate[(gi, th, ts, f)] = 0
                            continue
                        wt("v", S, S_gelu2)
                        if th == 0 and ts == 0 and f == 0:
                            wt("v", V, V_gbe[e_])
                        V.inc()
                        em("v", lambda e, a=bfv(ASB + f * 1024 + ts * 512, 512),
                               g=bfv(GBE + th * 1024 + ts * 512, 512):
                           e.tensor_tensor(a, a, g, ALU.mult).then_inc(V.h, 1))
                        V_gate[(gi, th, ts, f)] = V.n
                if th == 1 and gi + 1 < len(granules):
                    W_w1[granules[gi + 1]] = w1_dma(*granules[gi + 1], T_u_last)
                for ts in range(2 if FFN_LVL >= 5 else 0):
                    for c in range(8):
                        pbank = pb[2 + ((ts * 8 + c) % 4)]
                        pkey = (gi, th, ts * 8 + c - 4)
                        if ts * 8 + c - 4 >= 0 and pkey in V_f2:
                            wt("t", V, V_f2[pkey])
                        else:
                            pk = (gi, th - 1, ts * 8 + c + 12) if th == 1 else (gi - 1, 1, ts * 8 + c + 12)
                            if pk in V_f2:
                                wt("t", V, V_f2[pk])
                        if th == 0 and ts == 0 and c == 0:
                            wt("t", W_w2[(e_, fh)])
                        if c == 0:
                            wt("t", V, V_gate[(gi, th, ts, 7)])
                        for kc in range(8):
                            T.inc()
                            em("t", lambda e, l=bfv(W2B + kc * 1024 + c * 128, 128),
                                   rh=bfv(ASB + kc * 1024 + ts * 512, 512),
                                   o=pbank[:, :], st=(kc == 0), sp=(kc == 7):
                               e.matmul(o, l, rh, start=st, stop=sp).then_inc(T.h, 1))
                        T_f2_last = T.n
                        Tf = T.n
                        wt("v", T, Tf)
                        V.inc()
                        em("v", lambda e, i=pbank[:, :],
                               o=fpv(OACC[c] + th * 1024 + ts * 512, 512):
                           e.tensor_tensor(o, i, o, ALU.add).then_inc(V.h, 1))
                        V_f2[(gi, th, ts * 8 + c)] = V.n
                if th == 1 and gi + 1 < len(granules):
                    W_w2[granules[gi + 1]] = w2_dma(*granules[gi + 1], T_f2_last)
            barrier()
        V_ffn_done = V.n

        V_tadd = [0] * 16
        ST_st = [None] * 16
        V_tc = {}
        for b in range(16):
            w = (V, V_tadd[b - 2]) if b >= 2 else None
            Xr = ydma(xm[b * 128:(b + 1) * 128, :], fpv(XRL[b % 2], 1024), wait=w)
            if b >= 2:
                wt("y", ST_st[b - 2])
            for c in range(8):
                pbank = pb[6 + (c % 2)]
                if b == 0 and c == 0:
                    wt("t", V, V_f2.get((15, 0, 15), V_ffn_done))
                elif b == 8 and c == 0:
                    wt("t", V, V_ffn_done)
                pk = (b, c - 2) if c >= 2 else (b - 1, c + 6)
                if pk in V_tc:
                    wt("t", V, V_tc[pk])
                T.inc()
                em("t", lambda e, l=fpv(OACC[c] + b * 128, 128), rh=fpv(EYEF, 128),
                       o=pbank[:, 0:128]:
                   e.matmul(o, l, rh, start=True, stop=True).then_inc(T.h, 1))
                Tt = T.n
                wt("v", T, Tt)
                if c == 0:
                    wt("v", Xr)
                V.inc()
                em("v", lambda e, i=pbank[:, 0:128], o=fpv(XRL[b % 2] + c * 128, 128):
                   e.tensor_tensor(o, i, o, ALU.add).then_inc(V.h, 1))
                V_tc[(b, c)] = V.n
            V_tadd[b] = V.n
            ST_st[b] = stdma(fpv(XRL[b % 2], 1024), y[b * 128:(b + 1) * 128, :],
                             wait=(V, V_tadd[b]))
        for s in STP:
            wt("y", s, s.n)

        @block.sync
        def _(eng):
            for f in streams["y"]:
                f(eng)

        @block.gpsimd
        def _(eng):
            for f in streams["g"]:
                f(eng)

        @block.tensor
        def _(eng):
            for f in streams["t"]:
                f(eng)

        @block.vector
        def _(eng):
            for f in streams["v"]:
                f(eng)

        @block.scalar
        def _(eng):
            for f in streams["s"]:
                f(eng)

    return nc


def _esel8():
    s = np.zeros((8, 1024), np.float32)
    for e in range(8):
        s[e, e * 128:(e + 1) * 128] = 1.0
    return s.astype(BF)


def _consts():
    masks = np.zeros((5, 128, 512), np.float32)
    for m in range(5):
        kk = np.arange(128)[:, None]
        qq = np.arange(512)[None, :]
        masks[m] = (np.abs(128 * m + kk - qq - 4) <= 3).astype(np.float32)
    return {
        "masks": masks.astype(BF),
        "eye_b": np.eye(128).astype(BF),
        "eye_f": np.eye(128, dtype=np.float32),
        "ones_cb": np.ones((128, 1)).astype(BF),
        "ones_rb": np.ones((1, 128)).astype(BF),
        "ones_rf": np.ones((1, 128), np.float32),
        "pat864": np.tile(np.eye(8, dtype=np.float32), (1, 8)).reshape(8, 8, 8).transpose(0, 2, 1).reshape(8, 64),
        "grp648": np.repeat(np.eye(8, dtype=np.float32), 8, axis=0),
        "grp864": np.repeat(np.eye(8, dtype=np.float32), 8, axis=0).T.copy(),
        "eyee": np.eye(8, dtype=np.float32),
        "tile648": np.tile(np.eye(8, dtype=np.float32), (8, 1)),
        "esel8": _esel8(),
    }


def kernel(L0_attn_x, L0_attn_w_qkv, L0_attn_proj, L0_attn_w_out, L0_attn_b_out,
           L0_ffn_Wg, L0_ffn_W1, L0_ffn_W2):
    x = np.ascontiguousarray(np.asarray(L0_attn_x, dtype=np.float32)).reshape(16384, 1024)
    consts = _consts()
    pat = np.zeros((8, 64), np.float32)
    for r in range(8):
        pat[:, r * 8:(r + 1) * 8] = np.eye(8)
    consts["pat864"] = pat
    base = dict(
        wqkv=np.ascontiguousarray(L0_attn_w_qkv, dtype=np.float32),
        proj=np.ascontiguousarray(L0_attn_proj, dtype=np.float32),
        wout=np.ascontiguousarray(L0_attn_w_out, dtype=np.float32),
        boutT=np.ascontiguousarray(np.asarray(L0_attn_b_out, dtype=np.float32)
                                   .reshape(8, 128).T),
        wg=np.ascontiguousarray(L0_ffn_Wg, dtype=np.float32),
        w1=np.ascontiguousarray(L0_ffn_W1, dtype=np.float32),
        w2=np.ascontiguousarray(L0_ffn_W2, dtype=np.float32),
        **consts,
    )
    in_maps = []
    for c in range(NC):
        start = c * TOK
        seqlo = 0 if c < 4 else 8192
        seqhi = seqlo + 8192
        m = dict(base)
        m["xm"] = x[start:start + TOK]
        m["xhl"] = (x[start - 3:start] if start > seqlo
                    else np.zeros((3, 1024), np.float32))
        m["xhr"] = (x[start + TOK:start + TOK + 3] if start + TOK < seqhi
                    else np.zeros((3, 1024), np.float32))
        hm = np.ones((64, 6), np.float32)
        if start == seqlo:
            hm[:, 0:3] = 0.0
        if start + TOK == seqhi:
            hm[:, 3:6] = 0.0
        m["hm"] = hm
        so = np.zeros((64, 8), np.float32)
        so[c * 8:(c + 1) * 8, :] = np.eye(8)
        m["selown"] = so
        in_maps.append(m)
    nc = build_nc()
    res = run_bass_kernel_spmd(nc, in_maps, core_ids=list(range(NC)))
    out = np.concatenate([res.results[c]["y"] for c in range(NC)], axis=0)
    if DEBUG_DUMPS:
        import pickle
        dd = {k: np.stack([np.asarray(res.results[c][k]).astype(np.float32)
                           for c in range(NC)])
              for k in res.results[0] if k.startswith("dbg_")}
        with open("/root/problem/scratch/dev_dumps.pkl", "wb") as f:
            pickle.dump(dd, f)
    return out.reshape(2, 8192, 1024).astype(np.float32)


# version 14
# speedup vs baseline: 1.0084x; 1.0084x over previous
import numpy as np
import ml_dtypes
from contextlib import ExitStack

import concourse.bass as bass
import concourse.mybir as mybir
from concourse.bass_utils import run_bass_kernel_spmd

F32 = mybir.dt.float32
BF16 = mybir.dt.bfloat16
AF = mybir.ActivationFunctionType
ALU = mybir.AluOpType
AX = mybir.AxisListType
BF = ml_dtypes.bfloat16

TOK = 2048
PW = 2176
OW0 = 4
D = 1024
H = 8
DH = 128
FEAT = 64
E = 8
FF = 2048
NC = 8
BISECT = 28
CAP = 2048

XN = [c * PW for c in range(8)]
MASK0 = 17408
EYEB = MASK0 + 2560
ONES128B = EYEB + 128
ONES1B = ONES128B + 16
PROJB = ONES1B + 128
SWG = PROJB + 64
ESEL8 = SWG + 64
BFDUM = ESEL8 + 1024
RA = BFDUM + 2048
WHEAD = RA
WQKP = RA + 3072
TRB = RA + 4096
VTM = RA + 4352
SST = RA + 6528
QPHI = RA + 9088
KPHI = RA + 11264
OALL = RA + 13440
A_END = OALL + 16384
W1B = RA
W2B = RA + 8192
ASB = RA + 16384
GBE = RA + 24576
B_END = GBE + 2048
NBF = max(A_END, B_END)

EYEF = 0
ONES1F = 128
BOUTT = 256
HM = 264
SELOWN = 280
PAT864 = 288
GRP648 = 352
GRP864 = 360
EYEE = 424
TILE648 = 432
STATS = 440
SMALL = 584
LO8, HI8, TH8, CNT64, ISGE, ISLT, TMP8, TH64 = (SMALL + i for i in range(8))
EPSC = SMALL + 8
CNT8 = SMALL + 9
PEM = 600
MPR = PEM + 2048
RMX = MPR + 2048
BIG = RMX + 2048
OACC = [BIG + c * 2048 for c in range(8)]
XT = [BIG, BIG + 1024]
SQD = BIG + 2048
AGC = BIG + 3072
SCR = BIG + 5120
LGT = BIG + 7168
SCA = BIG + 9216
SCB = BIG + 9728
ZSC = BIG + 10240
ZBS = BIG + 10752
D8S = BIG + 11264
BIGEND = BIG + 16384
XRL = [BIGEND, BIGEND + 1024]
GSC = BIGEND + 2048
NFP = BIGEND + 2560

GELU_BF = True
DEBUG_DUMPS = False
STOP_AFTER = None
HEAD_STAGE = None
GLVL = 9
SERIAL = 0


class SemC:
    def __init__(self, h):
        self.h = h
        self.n = 0

    def inc(self, k=1):
        self.n += k
        return self.n


def build_nc():
    nc = bass.Bass("TRN2", target_bir_lowering=False, debug=False,
                   detect_race_conditions=False)
    dt = nc.dram_tensor
    xm = dt("xm", [TOK, D], F32, kind="ExternalInput").ap()
    xhl = dt("xhl", [3, D], F32, kind="ExternalInput").ap()
    xhr = dt("xhr", [3, D], F32, kind="ExternalInput").ap()
    wqkv = dt("wqkv", [D, 3 * D], F32, kind="ExternalInput").ap()
    proj = dt("proj", [DH, FEAT], F32, kind="ExternalInput").ap()
    wout = dt("wout", [D, D], F32, kind="ExternalInput").ap()
    boutT = dt("boutT", [128, 8], F32, kind="ExternalInput").ap()
    wg = dt("wg", [D, E], F32, kind="ExternalInput").ap()
    w1 = dt("w1", [E, D, FF], F32, kind="ExternalInput").ap()
    w2 = dt("w2", [E, FF, D], F32, kind="ExternalInput").ap()
    masks = dt("masks", [5, 128, 512], BF16, kind="ExternalInput").ap()
    eye_b = dt("eye_b", [128, 128], BF16, kind="ExternalInput").ap()
    eye_f = dt("eye_f", [128, 128], F32, kind="ExternalInput").ap()
    ones_cb = dt("ones_cb", [128, 1], BF16, kind="ExternalInput").ap()
    ones_rb = dt("ones_rb", [1, 128], BF16, kind="ExternalInput").ap()
    ones_rf = dt("ones_rf", [1, 128], F32, kind="ExternalInput").ap()
    hm = dt("hm", [64, 6], F32, kind="ExternalInput").ap()
    selown = dt("selown", [64, 8], F32, kind="ExternalInput").ap()
    pat864 = dt("pat864", [8, 64], F32, kind="ExternalInput").ap()
    grp648 = dt("grp648", [64, 8], F32, kind="ExternalInput").ap()
    grp864 = dt("grp864", [8, 64], F32, kind="ExternalInput").ap()
    eyee = dt("eyee", [8, 8], F32, kind="ExternalInput").ap()
    tile648 = dt("tile648", [64, 8], F32, kind="ExternalInput").ap()
    esel8 = dt("esel8", [8, 1024], BF16, kind="ExternalInput").ap()
    y = dt("y", [TOK, D], F32, kind="ExternalOutput").ap()
    if DEBUG_DUMPS:
        dbg_pem = dt("dbg_pem", [8, TOK], F32, kind="ExternalOutput").ap()
        dbg_mpr = dt("dbg_mpr", [8, TOK], F32, kind="ExternalOutput").ap()
        dbg_rmx = dt("dbg_rmx", [1, TOK], F32, kind="ExternalOutput").ap()
        dbg_lo = dt("dbg_lo", [8, 1], F32, kind="ExternalOutput").ap()
        dbg_attn = dt("dbg_attn", [128, 8 * TOK], F32, kind="ExternalOutput").ap()
        dbg_oall = dt("dbg_oall", [128, 8 * TOK], BF16, kind="ExternalOutput").ap()
        dbg_xn = dt("dbg_xn", [128, 8 * PW], BF16, kind="ExternalOutput").ap()
        dbg_qphi = dt("dbg_qphi", [64, 2054], BF16, kind="ExternalOutput").ap()
        dbg_st = dt("dbg_st", [128, 144], F32, kind="ExternalOutput").ap()
        dbg_kphi = dt("dbg_kphi", [64, 2054], BF16, kind="ExternalOutput").ap()

    cc_in = dt("cc_in", [8, TOK], F32, kind="Internal").ap()
    cc_out = dt("cc_out", [64, TOK], F32, kind="Internal").ap()

    with ExitStack() as ctx:
        bfa = ctx.enter_context(nc.sbuf_tensor("bfa", [128, NBF], BF16))
        fpa = ctx.enter_context(nc.sbuf_tensor("fpa", [128, NFP], F32))
        pb = [ctx.enter_context(nc.psum_tensor(f"pb{i}", [128, 512], F32))
              for i in range(8)]
        YP = [SemC(ctx.enter_context(nc.semaphore(f"Y{i}"))) for i in range(4)]
        GP = [SemC(ctx.enter_context(nc.semaphore(f"Gq{i}"))) for i in range(4)]
        STP = [SemC(ctx.enter_context(nc.semaphore(f"SQ{i}"))) for i in range(2)]
        ydma_i = [0]
        gdma_i = [0]
        stdma_i = [0]
        CC = SemC(ctx.enter_context(nc.semaphore("CC")))
        T = SemC(ctx.enter_context(nc.semaphore("T")))
        V = SemC(ctx.enter_context(nc.semaphore("V")))
        S = SemC(ctx.enter_context(nc.semaphore("S")))
        G = SemC(ctx.enter_context(nc.semaphore("G")))
        block = ctx.enter_context(nc.Block())

        streams = {"t": [], "v": [], "s": [], "g": [], "y": []}

        def em(eng, f):
            streams[eng].append(f)

        def barrier():
            if not SERIAL:
                return
            sems = [T, V, S, G] + YP + GP + STP + [CC]
            for eng in ("t", "v", "s", "g", "y"):
                for sm in sems:
                    if sm.n > 0:
                        wt(eng, sm, sm.n)

        def wt(eng, sem, val=None):
            if val is None:
                sem, val = sem
            if val > 0:
                em(eng, lambda e, h=sem.h, v=val: e.wait_ge(h, v))

        def vsync():
            em("v", lambda e, v=V.n: e.wait_ge(V.h, v))

        def wtall(eng, snap):
            for sem, val in snap:
                wt(eng, sem, val)

        def bfv(col, w, p0=0, p1=128):
            return bfa[p0:p1, col:col + w]

        def fpv(col, w, p0=0, p1=128):
            return fpa[p0:p1, col:col + w]

        def _pool_dma(stream, pool, idx, src, dst_view, wait):
            sem = pool[idx[0] % len(pool)]
            idx[0] += 1
            if wait is not None:
                wt(stream, wait)
            if sem.n > 0:
                wt(stream, sem, sem.n)
            n = sem.inc(16)
            em(stream, lambda e, s=src, d=dst_view, h=sem.h:
               e.dma_start(out=d, in_=s).then_inc(h, 16))
            return (sem, n)

        def ydma(src, dst_view, wait=None):
            return _pool_dma("y", YP, ydma_i, src, dst_view, wait)

        def gdma(src, dst_view, wait=None):
            return _pool_dma("g", GP, gdma_i, src, dst_view, wait)

        def stdma(src_view, dst, wait=None):
            return _pool_dma("y", STP, stdma_i, src_view, dst, wait)

        ydma(eye_f, fpv(EYEF, 128))
        ydma(ones_rf, fpa[0:1, ONES1F:ONES1F + 128])
        ydma(boutT, fpv(BOUTT, 8))
        ydma(hm, fpa[0:64, HM:HM + 6])
        ydma(selown, fpa[0:64, SELOWN:SELOWN + 8])
        ydma(pat864, fpa[0:8, PAT864:PAT864 + 64])
        ydma(grp648, fpa[0:64, GRP648:GRP648 + 8])
        ydma(grp864, fpa[0:8, GRP864:GRP864 + 64])
        ydma(eyee, fpa[0:8, EYEE:EYEE + 8])
        ydma(tile648, fpa[0:64, TILE648:TILE648 + 8])
        ydma(esel8, bfa[0:8, ESEL8:ESEL8 + 1024])
        ydma(masks.rearrange("m p f -> p m f"),
             bfv(MASK0, 2560).rearrange("p (m f) -> p m f", m=5))
        ydma(eye_b, bfv(EYEB, 128))
        ydma(ones_cb, bfv(ONES128B, 1))
        ydma(ones_rb, bfa[0:1, ONES1B:ONES1B + 128])
        X_consts = [(s, s.n) for s in YP if s.n > 0]

        W_proj = gdma(proj, bfv(PROJB, 64))
        W_wg = gdma(wg.rearrange("(c p) e -> p c e", p=128),
                    bfv(SWG, 64).rearrange("p (c e) -> p c e", c=8))
        def whead_dma(h, wait_t):
            toks = []
            for i, base in enumerate((0, D, 2 * D)):
                toks.append(gdma(
                    wqkv[:, base + h * DH:base + (h + 1) * DH]
                    .rearrange("(c p) f -> p c f", p=128),
                    bfv(WHEAD + i * 1024, 1024).rearrange("p (c f) -> p c f", c=8),
                    wait=(T, wait_t) if i == 0 and wait_t else None))
            return toks
        W_whead = [None] * (H + 1)
        W_whead[0] = whead_dma(0, 0)
        def gmemset(view):
            G.inc(1)
            em("g", lambda e, d=view: e.memset(d, 0.0).then_inc(G.h, 1))
        gmemset(bfa[0:64, KPHI + 2056:KPHI + PW])
        for c in range(8):
            gmemset(bfa[:, XN[c]:XN[c] + 1])
            gmemset(bfa[:, XN[c] + 2055:XN[c] + PW])
        G_zeros = G.n

        SWOUT_LO = RA
        SWOUT_HI = RA + 4352
        SWOUT_COL = [SWOUT_LO + kc * 1024 if kc < 4 else SWOUT_HI + (kc - 4) * 1024
                     for kc in range(8)]
        W_swout = [None, None]

        V_xn = [0] * 18
        S_trev = [[0] * 8 for _ in range(18)]
        T_tr = [[0] * 8 for _ in range(18)]
        X_xt = [0] * 18

        def tile_rows(t):
            return 3 if t >= 16 else 128

        def tile_dst_col(t):
            if t < 16:
                return OW0 + 128 * t
            return 1 if t == 16 else 2052

        V.inc()
        em("v", lambda e, o=fpv(EPSC, 1): e.memset(o, 1e-5).then_inc(V.h, 1))
        for t in range(18):
            src = xm[t * 128:(t + 1) * 128, :] if t < 16 else (xhl if t == 16 else xhr)
            r = tile_rows(t)
            X_xt[t] = ydma(src, fpv(XT[t % 2], 1024, 0, r),
                           wait=(V, V_xn[t - 2]) if t >= 2 else None)
            xt = fpv(XT[t % 2], 1024, 0, r)
            stc = STATS + 8 * t
            wt("s", X_xt[t])
            S.inc()
            em("s", lambda e, xv=xt, o=fpv(SQD, 1024, 0, r), a=fpv(stc + 1, 1, 0, r):
               e.activation(o, xv, AF.Square, accum_out=a).then_inc(S.h, 1))
            S.inc()
            em("s", lambda e, xv=xt, o=fpv(SQD, 1024, 0, r), a=fpv(stc + 0, 1, 0, r):
               e.activation(o, xv, AF.Copy, accum_out=a).then_inc(S.h, 1))
            S_sq = S.n
            wt("v", S, S_sq)
            for fdef in (
                lambda e, i=fpv(stc, 1, 0, r), o=fpv(stc + 2, 1, 0, r):
                    e.tensor_scalar_mul(o, i, -1.0 / D),
                lambda e, a=fpv(stc, 1, 0, r), b=fpv(stc + 2, 1, 0, r),
                       o=fpv(stc + 3, 1, 0, r):
                    e.tensor_mul(o, a, b),
                lambda e, i=fpv(stc + 1, 1, 0, r), o=fpv(stc + 4, 1, 0, r):
                    e.tensor_scalar_mul(o, i, 1.0 / D),
                lambda e, a=fpv(stc + 3, 1, 0, r), b=fpv(stc + 4, 1, 0, r),
                       o=fpv(stc + 5, 1, 0, r):
                    e.scalar_tensor_tensor(o, a, 1.0 / D, b, ALU.mult, ALU.add),
            ):
                vsync()
                V.inc()
                em("v", lambda e, f=fdef: f(e).then_inc(V.h, 1))
            V_var = V.n
            wt("s", V, V_var)
            S.inc()
            em("s", lambda e, i=fpv(stc + 5, 1, 0, r), o=fpv(stc + 6, 1, 0, r),
                   b=fpv(EPSC, 1, 0, r):
               e.activation(o, i, AF.Sqrt, bias=b).then_inc(S.h, 1))
            S_rs = S.n
            wt("v", S, S_rs)
            V.inc()
            em("v", lambda e, i=fpv(stc + 6, 1, 0, r), o=fpv(stc + 7, 1, 0, r):
               e.reciprocal(o, i).then_inc(V.h, 1))
            vsync()
            xtm_col = SST + (t % 2) * 1024
            if t >= 2:
                wt("v", T, T_tr[t - 2][7])
            V.inc()
            em("v", lambda e, xv=xt, m=fpv(stc + 2, 1, 0, r), sdv=fpv(stc + 7, 1, 0, r),
                   o=bfv(xtm_col, 1024, 0, r):
               e.tensor_scalar(o, xv, m, sdv, ALU.add, ALU.mult).then_inc(V.h, 1))
            V_xn[t] = V.n
            dst = tile_dst_col(t)
            for c in range(8):
                pbank = pb[6 + ((t * 8 + c) % 2)]
                prev = (t * 8 + c) - 2
                if prev >= 0:
                    wt("t", S, S_trev[prev // 8][prev % 8])
                if c == 0:
                    wt("t", V, V_xn[t])
                    if t == 0:
                            wtall("t", X_consts)
                T.inc()
                if r == 128:
                    em("t", lambda e, l=bfv(xtm_col + c * 128, 128), rh=bfv(EYEB, 128),
                           o=pbank[:, 0:128]:
                       e.matmul(o, l, rh, start=True, stop=True).then_inc(T.h, 1))
                else:
                    em("t", lambda e, l=bfv(xtm_col + c * 128, 128, 0, 3),
                           rh=bfv(EYEB, 3, 0, 3), o=pbank[:, 0:3]:
                       e.matmul(o, l, rh, start=True, stop=True).then_inc(T.h, 1))
                T_tr[t][c] = T.n
                wt("s", T, T_tr[t][c])
                wd = 3 if r == 3 else 128
                S.inc()
                em("s", lambda e, i=pbank[:, 0:wd], o=bfv(XN[c] + dst, wd), wd=wd:
                   e.copy(o, i).then_inc(S.h, 1))
                S_trev[t][c] = S.n
        S_ln_done = S.n
        barrier()
        S_xn_main = S_trev[15][7]

        V_lgt = [0] * 4
        for ts in range(4):
            wt("t", S, S_ln_done)
            if ts == 0:
                wt("t", W_wg)
                wt("t", W_proj)
            if ts > 0:
                wt("t", V, V_lgt[ts - 1])
            for kc in range(8):
                T.inc()
                em("t", lambda e, l=bfv(SWG + kc * 8, 8), rh=bfv(XN[kc] + OW0 + ts * 512, 512),
                       o=pb[7][0:8, :], st=(kc == 0), sp=(kc == 7):
                   e.matmul(o, l, rh, start=st, stop=sp).then_inc(T.h, 1))
            wt("v", T, T.n)
            V.inc()
            em("v", lambda e, i=pb[7][0:8, :], o=fpv(LGT + ts * 512, 512, 0, 8):
               e.tensor_copy(o, i).then_inc(V.h, 1))
            V_lgt[ts] = V.n
        wt("y", V, V_lgt[3])
        CC.inc(16)
        em("y", lambda e: e.dma_start(out=cc_in, in_=fpv(LGT, 2048, 0, 8)).then_inc(CC.h, 16))
        wt("g", CC, 16)
        CC.inc(1)
        em("g", lambda e: e.collective_compute(
            "AllGather", ALU.bypass, replica_groups=[list(range(NC))],
            ins=[cc_in], outs=[cc_out]).then_inc(CC.h, 1))
        wt("y", CC, 17)
        CC.inc(16)
        em("y", lambda e: e.dma_start(out=fpv(AGC, 2048, 0, 64), in_=cc_out).then_inc(CC.h, 16))
        wt("s", CC, 33)
        S.inc()
        em("s", lambda e, v=fpv(AGC, 2048, 0, 64): e.activation(v, v, AF.Exp).then_inc(S.h, 1))
        S_exp = S.n
        V_sm = [0] * 4
        for ts in range(4):
            wt("t", S, S_exp)
            if ts > 0:
                wt("t", V, V_sm[ts - 1])
            T.inc()
            em("t", lambda e, l=fpa[0:64, GRP648:GRP648 + 8],
                   rh=fpv(AGC + ts * 512, 512, 0, 64), o=pb[7][0:8, :]:
               e.matmul(o, l, rh, start=True, stop=True).then_inc(T.h, 1))
            wt("v", T, T.n)
            V.inc()
            em("v", lambda e, i=pb[7][0:8, :], o=fpv(D8S, 512, 0, 8):
               e.reciprocal(o, i).then_inc(V.h, 1))
            Vr = V.n
            wt("t", V, Vr)
            T.inc()
            em("t", lambda e, l=fpa[0:8, GRP864:GRP864 + 64], rh=fpv(D8S, 512, 0, 8),
                   o=pb[7][0:64, :]:
               e.matmul(o, l, rh, start=True, stop=True).then_inc(T.h, 1))
            wt("v", T, T.n)
            V.inc()
            em("v", lambda e, p=fpv(AGC + ts * 512, 512, 0, 64), i=pb[7][0:64, :]:
               e.tensor_tensor(p, p, i, ALU.mult).then_inc(V.h, 1))
            V_sm[ts] = V.n
        V_probs = V.n
        barrier()
        for col, val in ((LO8, 0.0), (HI8, 1.0)):
            V.inc()
            em("v", lambda e, o=fpv(col, 1, 0, 8), vv=val: e.memset(o, vv).then_inc(V.h, 1))
        V_bis_prev = [V.n]
        T_bis_prev = [0]
        DO_HEADS = STOP_AFTER not in ("ln", "route")
        H_CUT = H
        if STOP_AFTER and STOP_AFTER.startswith("h"):
            H_CUT = int(STOP_AFTER[1:])
        DO_OUTPROJ = STOP_AFTER in (None, "attn", "bis") and DO_HEADS
        DO_BISECT = STOP_AFTER in (None, "bis")
        DO_FFN = STOP_AFTER is None or (STOP_AFTER or "").startswith("ffn")
        FFN_GR = 16
        FFN_LVL = 9
        if STOP_AFTER and STOP_AFTER.startswith("ffn"):
            parts = STOP_AFTER[3:].split("_")
            FFN_GR = int(parts[0])
            if len(parts) > 1:
                FFN_LVL = int(parts[1])

        bis_state = {"k": 0, "counted": 0}

        def emit_bisect_theta():
            V.inc()
            em("v", lambda e, o=fpv(TH8, 1, 0, 8), a=fpv(LO8, 1, 0, 8), b=fpv(HI8, 1, 0, 8):
               e.tensor_add(o, a, b).then_inc(V.h, 1))
            vsync()
            V.inc()
            em("v", lambda e, o=fpv(TH8, 1, 0, 8):
               e.tensor_scalar_mul(o, o, 0.5).then_inc(V.h, 1))
            bis_state["thV"] = V.n

        def emit_bisect_phaseA():
            if "thV" not in bis_state:
                emit_bisect_theta()
            Vth = bis_state.pop("thV")
            if "cntS" in bis_state:
                wt("t", S, bis_state["cntS"])
            wt("t", V, Vth)
            T.inc()
            em("t", lambda e, l=fpa[0:8, PAT864:PAT864 + 64], rh=fpv(TH8, 1, 0, 8),
                   o=pb[7][0:64, 0:1]:
               e.matmul(o, l, rh, start=True, stop=True).then_inc(T.h, 1))
            T_bis_prev[0] = T.n
            wt("s", T, T.n)
            S.inc()
            em("s", lambda e, i=pb[7][0:64, 0:1], o=fpv(TH64, 1, 0, 64):
               e.copy(o, i).then_inc(S.h, 1))
            Sc = S.n
            wt("v", S, Sc)
            V.inc()
            em("v", lambda e, i=fpv(AGC, 2048, 0, 64), th=fpv(TH64, 1, 0, 64),
                   o=bfv(BFDUM, 2048, 0, 64), a=fpv(CNT64, 1, 0, 64):
               e.tensor_scalar(o, i, th, 0.0, ALU.is_gt, ALU.add,
                               accum_out=a).then_inc(V.h, 1))
            bis_state["cmpV"] = V.n

        def emit_bisect_phaseB():
            wt("t", V, bis_state["cmpV"])
            T.inc()
            em("t", lambda e, l=fpa[0:64, TILE648:TILE648 + 8], rh=fpv(CNT64, 1, 0, 64),
                   o=pb[7][0:8, 4:5]:
               e.matmul(o, l, rh, start=True, stop=True).then_inc(T.h, 1))
            wt("s", T, T.n)
            S.inc()
            em("s", lambda e, i=pb[7][0:8, 4:5], o=fpv(CNT8, 1, 0, 8):
               e.copy(o, i).then_inc(S.h, 1))
            Sc2 = S.n
            bis_state["cntS"] = S.n
            wt("v", S, Sc2)
            for fdef in (
                lambda e: e.tensor_scalar(fpv(ISGE, 1, 0, 8), fpv(CNT8, 1, 0, 8),
                                          CAP - 0.5, None, ALU.is_gt),
                lambda e: e.tensor_sub(fpv(TMP8, 1, 0, 8), fpv(TH8, 1, 0, 8), fpv(LO8, 1, 0, 8)),
                lambda e: e.tensor_mul(fpv(TMP8, 1, 0, 8), fpv(TMP8, 1, 0, 8), fpv(ISGE, 1, 0, 8)),
                lambda e: e.tensor_add(fpv(LO8, 1, 0, 8), fpv(LO8, 1, 0, 8), fpv(TMP8, 1, 0, 8)),
                lambda e: e.tensor_scalar(fpv(ISLT, 1, 0, 8), fpv(ISGE, 1, 0, 8), -1.0, -1.0,
                                          ALU.add, ALU.mult),
                lambda e: e.tensor_sub(fpv(TMP8, 1, 0, 8), fpv(TH8, 1, 0, 8), fpv(HI8, 1, 0, 8)),
                lambda e: e.tensor_mul(fpv(TMP8, 1, 0, 8), fpv(TMP8, 1, 0, 8), fpv(ISLT, 1, 0, 8)),
                lambda e: e.tensor_add(fpv(HI8, 1, 0, 8), fpv(HI8, 1, 0, 8), fpv(TMP8, 1, 0, 8)),
            ):
                vsync()
                V.inc()
                em("v", lambda e, f=fdef: f(e).then_inc(V.h, 1))
            V_bis_prev[0] = V.n
            wt("v", T, T_bis_prev[0])
            emit_bisect_theta()

        KOFF = [0, 512, 1024, 1536, 2048]
        KWID = [512, 512, 512, 512, 8]
        bis_emitted = [0]
        prev_szb = [0]
        pend = [None]

        def flush_oevac():
            if pend[0] is None:
                return
            hh, gg, Vz, TO, pO = pend[0]
            pend[0] = None
            wt("t", V, Vz)
            if prev_szb[0]:
                wt("t", S, prev_szb[0])
            T.inc()
            em("t", lambda e, l=fpa[0:1, ONES1F:ONES1F + 128], rh=fpv(ZSC, 512, 0, 1),
                   o=pb[4][:, :]:
               e.matmul(o, l, rh, start=True, stop=True).then_inc(T.h, 1))
            Tzb = T.n
            wt("s", T, Tzb)
            if V_oev_last[0]:
                wt("s", V, V_oev_last[0])
            S.inc()
            em("s", lambda e, i=pb[4][:, :], o=fpv(ZBS, 512):
               e.copy(o, i).then_inc(S.h, 1))
            Szb = S.n
            prev_szb[0] = Szb
            wt("v", T, TO)
            wt("v", S, Szb)
            V.inc()
            em("v", lambda e, i=pO[:, :], z=fpv(ZBS, 512),
                   o=bfv(OALL + hh * 2048 + gg * 512, 512):
               e.tensor_tensor(o, i, z, ALU.mult).then_inc(V.h, 1))
            V_oev_prev[(hh, gg)] = V.n
            V_oev_last[0] = V.n

        V_oev_last = [0]
        S_vtm = [[0] * 17 for _ in range(H)]
        V_oall = [0] * H
        V_mask_prev = {}
        V_oev_prev = {}
        T_vlast = [0] * H
        T_head_last = [0] * H

        for h in range(H_CUT if DO_HEADS else 0):
            wtall("t", W_whead[h])
            wtall("t", X_consts)
            if h > 0:
                wt("t", V, V_oall[h - 1])
                wt("t", S, S.n)
            S_wqkp = 0
            Tp_prev = 0
            for kc in range(8):
                pt = pb[4 + (kc % 2)]
                po = pb[4 + ((kc + 1) % 2)]
                if S_wqkp:
                    wt("t", S, S_wqkp)
                S_tr = 0
                for qi in range(2):
                    if qi == 1:
                        wt("t", S, S_tr)
                    T.inc()
                    em("t", lambda e, l=bfv(WHEAD + qi * 1024 + kc * 128, 128),
                           rh=bfv(EYEB, 128), o=pt[:, 0:128]:
                       e.matmul(o, l, rh, start=True, stop=True).then_inc(T.h, 1))
                    Ttr = T.n
                    wt("s", T, Ttr)
                    if qi == 0 and Tp_prev:
                        wt("s", T, Tp_prev)
                    S.inc()
                    em("s", lambda e, i=pt[:, 0:128], o=bfv(TRB + qi * 128, 128):
                       e.copy(o, i).then_inc(S.h, 1))
                    S_tr = S.n
                wt("t", S, S_tr)
                for qi in range(2):
                    T.inc()
                    em("t", lambda e, l=bfv(TRB + qi * 128, 128), rh=bfv(PROJB, 64),
                           o=po[:, qi * 64:(qi + 1) * 64]:
                       e.matmul(o, l, rh, start=True, stop=True).then_inc(T.h, 1))
                Tp = T.n
                Tp_prev = Tp
                wt("s", T, Tp)
                S.inc()
                em("s", lambda e, i=po[:, 0:128], o=bfv(WQKP + kc * 128, 128):
                   e.copy(o, i).then_inc(S.h, 1))
                S_wqkp = S.n
            for b in range(0 if HEAD_STAGE == "pre" else 17):
                pbank = pb[b % 2]
                if b >= 2:
                    wt("t", S, S_vtm[h][b - 2])
                elif h > 0 and b < 2:
                    wt("t", S, S_vtm[h - 1][15 + b])
                if b == 0:
                    if h == 0:
                        wt("t", S, S_ln_done)
                        wt("t", G, G_zeros)
                for kc in range(8):
                    T.inc()
                    em("t", lambda e, l=bfv(XN[kc] + b * 128, 128),
                           rh=bfv(WHEAD + 2048 + kc * 128, 128),
                           o=pbank[:, 0:128], st=(kc == 0), sp=(kc == 7):
                       e.matmul(o, l, rh, start=st, stop=sp).then_inc(T.h, 1))
                T_vlast[h] = T.n
                wt("s", T, T.n)
                S.inc()
                em("s", lambda e, i=pbank[:, 0:128], o=bfv(VTM + b * 128, 128):
                   e.copy(o, i).then_inc(S.h, 1))
                S_vtm[h][b] = S.n
            if h + 1 < H:
                W_whead[h + 1] = whead_dma(h + 1, T_vlast[h])
            elif h == H - 1:
                pass
            S_phi = 0
            V_phi = 0
            qjobs = [(0, QPHI, OW0 + 512 * g, 512) for g in range(4)]
            kjobs = [(64, KPHI, o, 512) for o in (0, 512, 1024, 1536, 1664)]
            phi_jobs = [] if HEAD_STAGE in ("pre", "v") else qjobs + kjobs
            V_phi_list = [0] * len(phi_jobs)
            V_min_list = [0] * len(phi_jobs)
            S_relu_list = [0] * len(phi_jobs)
            for ji, (wcol, dstc, off, wd) in enumerate(phi_jobs):
                pbank = pb[2 + (ji % 2)]
                wt("t", S, S_wqkp)
                if ji >= 2:
                    wt("t", V, V_min_list[ji - 2])
                for kc in range(8):
                    T.inc()
                    em("t", lambda e, l=bfv(WQKP + kc * 128 + wcol, 64),
                           rh=bfv(XN[kc] + off, wd), o=pbank[0:64, 0:wd],
                           st=(kc == 0), sp=(kc == 7):
                       e.matmul(o, l, rh, start=st, stop=sp).then_inc(T.h, 1))
                Tph = T.n
                wt("v", T, Tph)
                V.inc()
                em("v", lambda e, i=pbank[0:64, 0:wd], o=fpv(SCA, wd, 0, 64):
                   e.tensor_scalar_max(o, i, 0.0).then_inc(V.h, 1))
                V.inc()
                em("v", lambda e, i=pbank[0:64, 0:wd], o=fpv(SCB, wd, 0, 64):
                   e.tensor_scalar_min(o, i, 0.0).then_inc(V.h, 1))
                V_min_list[ji] = V.n
                wt("s", V, V.n)
                S.inc()
                em("s", lambda e, o=fpv(SCB, wd, 0, 64):
                   e.activation(o, o, AF.Exp).then_inc(S.h, 1))
                S_phi = S.n
                wt("v", S, S_phi)
                V.inc()
                em("v", lambda e, a=fpv(SCA, wd, 0, 64), b=fpv(SCB, wd, 0, 64),
                       o=bfv(dstc + off, wd, 0, 64):
                   e.tensor_add(o, a, b).then_inc(V.h, 1))
                V_phi_list[ji] = V.n
                V_phi = V.n
            T_phi_last = T.n
            for (c0, w0, hmc) in (((1, 3, HM), (2052, 3, HM + 3))
                                  if HEAD_STAGE in (None, "phi") else ()):
                V.inc()
                em("v", lambda e, kv=bfv(KPHI + c0, w0, 0, 64),
                       m=fpv(hmc, w0, 0, 64):
                   e.tensor_tensor(kv, kv, m, ALU.mult).then_inc(V.h, 1))
            V_kphi = V.n
            for g in range(0 if HEAD_STAGE in ("pre", "v", "phi") else 4):
                Ts_tile = [0] * 5
                V_mask = [0] * 5
                for m in range(5):
                    pbank = pb[m % 3]
                    key = (h, g, m)
                    if m >= 3:
                        pkey = (h, g, m - 3)
                    elif g > 0:
                        pkey = (h, g - 1, m + 3 if m < 2 else 2)
                    else:
                        pkey = (h - 1, 3, m + 3 if m < 2 else 2)
                    if pkey in V_mask_prev:
                        wt("t", V, V_mask_prev[pkey])
                    if m == 0 and g == 0:
                        wt("t", V, V_kphi)
                        wt("t", S, S_vtm[h][16])
                    T.inc()
                    em("t", lambda e, l=bfv(KPHI + g * 512 + m * 128, 128, 0, 64),
                           rh=bfv(QPHI + OW0 + g * 512, 512, 0, 64), o=pbank[:, :]:
                       e.matmul(o, l, rh, start=True, stop=True).then_inc(T.h, 1))
                    Ts_tile[m] = T.n
                    wt("v", T, Ts_tile[m])
                    V.inc()
                    em("v", lambda e, i=pbank[:, :], mk=bfv(MASK0 + m * 512, 512),
                           o=bfv(SST + m * 512, 512):
                       e.tensor_tensor(o, i, mk, ALU.mult).then_inc(V.h, 1))
                    V_mask[m] = V.n
                    V_mask_prev[key] = V.n
                if DO_BISECT and bis_state["counted"] < bis_state["k"]:
                    emit_bisect_phaseB()
                    bis_state["counted"] += 1
                flush_oevac()
                wt("t", V, V_mask[4])
                for m in range(5):
                    T.inc()
                    em("t", lambda e, l=bfv(ONES128B, 1), rh=bfv(SST + m * 512, 512),
                           o=pb[3][0:1, :], st=(m == 0), sp=(m == 4):
                       e.matmul(o, l, rh, start=st, stop=sp).then_inc(T.h, 1))
                Tz = T.n
                wt("v", T, Tz)
                V.inc()
                em("v", lambda e, i=pb[3][0:1, :], o=fpv(ZSC, 512, 0, 1):
                   e.tensor_scalar_add(o, i, 1e-8).then_inc(V.h, 1))
                vsync()
                V.inc()
                em("v", lambda e, o=fpv(ZSC, 512, 0, 1): e.reciprocal(o, o).then_inc(V.h, 1))
                Vz = V.n
                pO = pb[5 + (g % 2)]
                okey = (h, g - 2) if g >= 2 else (h - 1, g + 2)
                if okey in V_oev_prev:
                    wt("t", V, V_oev_prev[okey])
                if g < 2:
                    wt("t", S, S_wqkp)
                for m in range(5):
                    T.inc()
                    em("t", lambda e, l=bfv(VTM + (g * 4 + m) * 128, 128),
                           rh=bfv(SST + m * 512, 512), o=pO[:, :],
                           st=(m == 0), sp=(m == 4):
                       e.matmul(o, l, rh, start=st, stop=sp).then_inc(T.h, 1))
                TO = T.n
                T_head_last[h] = T.n
                pend[0] = (h, g, Vz, TO, pO)
                if DO_BISECT and bis_state["counted"] < bis_state["k"]:
                    emit_bisect_phaseB()
                    bis_state["counted"] += 1
                if DO_BISECT and bis_state["k"] < BISECT:
                    emit_bisect_phaseA()
                    bis_state["k"] += 1
            flush_oevac()
            V_oall[h] = V.n if HEAD_STAGE is None else max(V.n, 1)
            barrier()
            if h == H - 1 and DO_OUTPROJ:
                for kc in range(4):
                    W_swout[0] = gdma(wout[kc * 128:(kc + 1) * 128, :],
                                      bfv(SWOUT_COL[kc], 1024),
                                      wait=(T, T_phi_last) if kc == 0 else None)
                for kc in range(4, 8):
                    W_swout[1] = gdma(wout[kc * 128:(kc + 1) * 128, :],
                                      bfv(SWOUT_COL[kc], 1024),
                                      wait=(T, T_head_last[h]) if kc == 4 else None)


        if DO_BISECT and bis_state["counted"] < bis_state["k"]:
            emit_bisect_phaseB()
            bis_state["counted"] += 1
        V_pem = [0] * 4
        for ts in range(4 if DO_BISECT else 0):
            wt("t", V, V_bis_prev[0])
            if ts > 0:
                wt("t", V, V_pem[ts - 1])
            T.inc()
            em("t", lambda e, l=fpa[0:64, SELOWN:SELOWN + 8],
                   rh=fpv(AGC + ts * 512, 512, 0, 64), o=pb[7][0:8, :]:
               e.matmul(o, l, rh, start=True, stop=True).then_inc(T.h, 1))
            wt("v", T, T.n)
            V.inc()
            em("v", lambda e, i=pb[7][0:8, :], o=fpv(PEM + ts * 512, 512, 0, 8):
               e.tensor_copy(o, i).then_inc(V.h, 1))
            V_pem[ts] = V.n
        if DO_BISECT:
            V.inc()
            em("v", lambda e, i=fpv(PEM, 2048, 0, 8), th=fpv(LO8, 1, 0, 8),
                   o=fpv(SCR, 2048, 0, 8):
               e.tensor_scalar(o, i, th, None, ALU.is_gt).then_inc(V.h, 1))
            V.inc()
            em("v", lambda e, s=fpv(SCR, 2048, 0, 8), p=fpv(PEM, 2048, 0, 8),
                   o=fpv(MPR, 2048, 0, 8):
               e.scalar_tensor_tensor(o, s, 2.0, p, ALU.mult, ALU.add).then_inc(V.h, 1))
        V_mpr = V.n
        V_rmx_done = 0
        for ts in range(4 if DO_BISECT else 0):
            for ee in range(8):
                pbank = pb[6 + (ee % 2)]
                wt("t", V, V_mpr if ee == 0 and ts == 0 else V.n)
                T.inc()
                em("t", lambda e, l=fpa[0:8, EYEE + ee:EYEE + ee + 1],
                       rh=fpv(MPR + ts * 512, 512, 0, 8), o=pbank[0:1, :]:
                   e.matmul(o, l, rh, start=True, stop=True).then_inc(T.h, 1))
                Tx = T.n
                wt("v", T, Tx)
                V.inc()
                if ee == 0:
                    em("v", lambda e, i=pbank[0:1, :], o=fpv(RMX + ts * 512, 512, 0, 1):
                       e.tensor_copy(o, i).then_inc(V.h, 1))
                else:
                    em("v", lambda e, i=pbank[0:1, :], o=fpv(RMX + ts * 512, 512, 0, 1):
                       e.tensor_tensor(o, o, i, ALU.max).then_inc(V.h, 1))
        V_rmx_done = V.n

        V_opj = {}
        V_opj_last = V.n
        T_opj_last = T.n
        for ts in range(4 if DO_OUTPROJ else 0):
            for c in range(8):
                pbank = pb[(ts * 8 + c) % 4]
                pkey = (ts * 8 + c) - 4
                if pkey >= 0:
                    wt("t", V, V_opj[pkey])
                if ts == 0 and c == 0:
                    wt("t", V, V_oall[H - 1])
                    wt("t", W_swout[0])
                for kc in range(8):
                    if ts == 0 and c == 0 and kc == 4:
                        wt("t", W_swout[1])
                    T.inc()
                    em("t", lambda e, l=bfv(SWOUT_COL[kc] + c * 128, 128),
                           rh=bfv(OALL + kc * 2048 + ts * 512, 512), o=pbank[:, :],
                           st=(kc == 0), sp=(kc == 7):
                       e.matmul(o, l, rh, start=st, stop=sp).then_inc(T.h, 1))
                Tp = T.n
                wt("v", T, Tp)
                V.inc()
                em("v", lambda e, i=pbank[:, :], b=fpv(BOUTT + c, 1),
                       o=fpv(OACC[c] + ts * 512, 512):
                   e.tensor_scalar(o, i, b, None, ALU.add).then_inc(V.h, 1))
                V_opj[ts * 8 + c] = V.n
        if DO_HEADS:
            T_opj_last = T.n
            V_opj_last = V.n
        dbg_toks = []
        if DEBUG_DUMPS:
            dbg_toks.append(ydma(fpv(BIG, 8 * TOK), dbg_attn, wait=(V, V_opj_last)))
            dbg_toks.append(ydma(bfv(OALL, 8 * TOK), dbg_oall))
            dbg_toks.append(ydma(bfv(0, 8 * PW), dbg_xn))
            dbg_toks.append(ydma(fpv(PEM, TOK, 0, 8), dbg_pem))
            dbg_toks.append(ydma(fpv(MPR, TOK, 0, 8), dbg_mpr))
            dbg_toks.append(ydma(fpv(RMX, TOK, 0, 1), dbg_rmx))
            dbg_toks.append(ydma(fpv(LO8, 1, 0, 8), dbg_lo))
            dbg_toks.append(ydma(bfv(QPHI + 1, 2054, 0, 64), dbg_qphi))
            dbg_toks.append(ydma(fpv(STATS, 144), dbg_st))
            dbg_toks.append(ydma(bfv(KPHI + 1, 2054, 0, 64), dbg_kphi))
            for tok in dbg_toks:
                wt("v", tok)
                wt("s", tok)

        def w1_dma(e_, fh, wait_t):
            return gdma(w1[e_, :, fh * 1024:(fh + 1) * 1024]
                        .rearrange("(kc p) f -> p kc f", p=128),
                        bfv(W1B, 8192).rearrange("p (kc f) -> p kc f", kc=8),
                        wait=(T, wait_t) if wait_t else None)

        def w2_dma(e_, fh, wait_t):
            return gdma(w2[e_, fh * 1024:(fh + 1) * 1024, :]
                        .rearrange("(kc p) f -> p kc f", p=128),
                        bfv(W2B, 8192).rearrange("p (kc f) -> p kc f", kc=8),
                        wait=(T, wait_t) if wait_t else None)

        granules = ([(e_, fh) for e_ in range(E) for fh in range(2)][:FFN_GR]
                    if DO_FFN else [])
        W_w1 = {}
        W_w2 = {}
        if DEBUG_DUMPS:
            for tok in dbg_toks:
                wt("g", tok)
        if DO_FFN:
            W_w1[granules[0]] = w1_dma(0, 0, T_opj_last)
            W_w2[granules[0]] = w2_dma(0, 0, T_opj_last)

        V_gem = 0
        if DO_BISECT:
            for ts in range(4):
                wt("t", V, V_rmx_done if ts == 0 else V_gem)
                T.inc()
                em("t", lambda e, l=fpa[0:1, ONES1F:ONES1F + 8],
                       rh=fpv(RMX + ts * 512, 512, 0, 1), o=pb[7][0:8, :]:
                   e.matmul(o, l, rh, start=True, stop=True).then_inc(T.h, 1))
                wt("s", T, T.n)
                S.inc()
                em("s", lambda e, i=pb[7][0:8, :], o=fpv(GSC, 512, 0, 8):
                   e.copy(o, i).then_inc(S.h, 1))
                wt("v", S, S.n)
                V.inc()
                em("v", lambda e, a=fpv(GSC, 512, 0, 8), m=fpv(MPR + ts * 512, 512, 0, 8):
                   e.tensor_tensor(a, m, a, ALU.is_ge).then_inc(V.h, 1))
                V.inc()
                em("v", lambda e, a=fpv(GSC, 512, 0, 8), p=fpv(PEM + ts * 512, 512, 0, 8),
                       o=bfv(BFDUM + ts * 512, 512, 0, 8):
                   e.tensor_tensor(o, a, p, ALU.mult).then_inc(V.h, 1))
                V_gem = V.n

        V_gbe = [0] * E
        v_gp_prev = [V_gem]

        def emit_gate_prep(e_):
            v_prev = v_gp_prev
            for ts in range(4):
                wt("t", V, max(V_gem, v_prev[0]))
                pbank = pb[6 + (ts % 2)]
                T.inc()
                em("t", lambda e, l=bfa[0:8, ESEL8 + e_ * 128:ESEL8 + (e_ + 1) * 128],
                       rh=bfv(BFDUM + ts * 512, 512, 0, 8), o=pbank[:, :]:
                   e.matmul(o, l, rh, start=True, stop=True).then_inc(T.h, 1))
                Tb = T.n
                wt("v", T, Tb)
                V.inc()
                em("v", lambda e, i=pbank[:, :], o=bfv(GBE + ts * 512, 512):
                   e.tensor_copy(o, i).then_inc(V.h, 1))
                v_prev[0] = V.n
            V_gbe[e_] = V.n

        T_u_last = 0
        T_f2_last = T_opj_last
        V_f2 = {}
        S_gelu = {}
        V_gate = {}
        for gi, (e_, fh) in enumerate(granules):
            if fh == 0 and FFN_LVL >= 3:
                emit_gate_prep(e_)
            for th in range(2):
                for ts in range(2):
                    for f in range(8):
                        pbank = pb[f % 2]
                        idx = (gi, th, ts, f)
                        pidx = (gi, th, ts, f - 2)
                        if f >= 2 and (gi, th, ts, f - 2) in S_gelu:
                            wt("t", S, S_gelu[(gi, th, ts, f - 2)])
                        elif ts == 0 and f < 2:
                            pk = (gi, th - 1, 1, 6 + f)
                            if pk in S_gelu:
                                wt("t", S, S_gelu[pk])
                            pk2 = (gi - 1, 1, 1, 6 + f)
                            if th == 0 and pk2 in S_gelu:
                                wt("t", S, S_gelu[pk2])
                        elif ts == 1 and f < 2:
                            pk = (gi, th, 0, 6 + f)
                            if pk in S_gelu:
                                wt("t", S, S_gelu[pk])
                        if th == 0 and ts == 0 and f == 0:
                            wt("t", W_w1[(e_, fh)])
                            if gi == 0:
                                wt("t", V, V_opj_last)
                        for kc in range(8):
                            T.inc()
                            em("t", lambda e, l=bfv(W1B + kc * 1024 + f * 128, 128),
                                   rh=bfv(XN[kc] + OW0 + th * 1024 + ts * 512, 512),
                                   o=pbank[:, :], st=(kc == 0), sp=(kc == 7):
                               e.matmul(o, l, rh, start=st, stop=sp).then_inc(T.h, 1))
                        T_u_last = T.n
                        Tu = T.n
                        if FFN_LVL < 2:
                            S_gelu[idx] = 0
                            V_gate[(gi, th, ts, f)] = 0
                            continue
                        if ts == 0 and f == 0:
                            wt("s", T, T_f2_last)
                        wt("s", T, Tu)
                        pkg = (gi, th - 1, ts, f) if th == 1 else (gi - 1, 1, ts, f)
                        if pkg in V_gate:
                            wt("s", V, V_gate[pkg])
                        S.inc()
                        em("s", lambda e, i=pbank[:, :], o=fpv(GSC, 512):
                           e.copy(o, i).then_inc(S.h, 1))
                        S_gelu[idx] = S.n
                        S.inc()
                        em("s", lambda e, i=fpv(GSC, 512),
                               o=bfv(ASB + f * 1024 + ts * 512, 512):
                           e.activation(o, i, AF.Gelu).then_inc(S.h, 1))
                        S_gelu2 = S.n
                        if FFN_LVL < 4:
                            V_gate[(gi, th, ts, f)] = 0
                            continue
                        wt("v", S, S_gelu2)
                        if th == 0 and ts == 0 and f == 0:
                            wt("v", V, V_gbe[e_])
                        V.inc()
                        em("v", lambda e, a=bfv(ASB + f * 1024 + ts * 512, 512),
                               g=bfv(GBE + th * 1024 + ts * 512, 512):
                           e.tensor_tensor(a, a, g, ALU.mult).then_inc(V.h, 1))
                        V_gate[(gi, th, ts, f)] = V.n
                if th == 1 and gi + 1 < len(granules):
                    W_w1[granules[gi + 1]] = w1_dma(*granules[gi + 1], T_u_last)
                for ts in range(2 if FFN_LVL >= 5 else 0):
                    for c in range(8):
                        pbank = pb[2 + ((ts * 8 + c) % 4)]
                        pkey = (gi, th, ts * 8 + c - 4)
                        if ts * 8 + c - 4 >= 0 and pkey in V_f2:
                            wt("t", V, V_f2[pkey])
                        else:
                            pk = (gi, th - 1, ts * 8 + c + 12) if th == 1 else (gi - 1, 1, ts * 8 + c + 12)
                            if pk in V_f2:
                                wt("t", V, V_f2[pk])
                        if th == 0 and ts == 0 and c == 0:
                            wt("t", W_w2[(e_, fh)])
                        if c == 0:
                            wt("t", V, V_gate[(gi, th, ts, 7)])
                        for kc in range(8):
                            T.inc()
                            em("t", lambda e, l=bfv(W2B + kc * 1024 + c * 128, 128),
                                   rh=bfv(ASB + kc * 1024 + ts * 512, 512),
                                   o=pbank[:, :], st=(kc == 0), sp=(kc == 7):
                               e.matmul(o, l, rh, start=st, stop=sp).then_inc(T.h, 1))
                        T_f2_last = T.n
                        Tf = T.n
                        wt("v", T, Tf)
                        V.inc()
                        em("v", lambda e, i=pbank[:, :],
                               o=fpv(OACC[c] + th * 1024 + ts * 512, 512):
                           e.tensor_tensor(o, i, o, ALU.add).then_inc(V.h, 1))
                        V_f2[(gi, th, ts * 8 + c)] = V.n
                if th == 1 and gi + 1 < len(granules):
                    W_w2[granules[gi + 1]] = w2_dma(*granules[gi + 1], T_f2_last)
            barrier()
        V_ffn_done = V.n

        V_tadd = [0] * 16
        ST_st = [None] * 16
        V_tc = {}
        for b in range(16):
            w = (V, V_tadd[b - 2]) if b >= 2 else None
            Xr = ydma(xm[b * 128:(b + 1) * 128, :], fpv(XRL[b % 2], 1024), wait=w)
            if b >= 2:
                wt("y", ST_st[b - 2])
            for c in range(8):
                pbank = pb[6 + (c % 2)]
                if b == 0 and c == 0:
                    wt("t", V, V_ffn_done)
                pk = (b, c - 2) if c >= 2 else (b - 1, c + 6)
                if pk in V_tc:
                    wt("t", V, V_tc[pk])
                T.inc()
                em("t", lambda e, l=fpv(OACC[c] + b * 128, 128), rh=fpv(EYEF, 128),
                       o=pbank[:, 0:128]:
                   e.matmul(o, l, rh, start=True, stop=True).then_inc(T.h, 1))
                Tt = T.n
                wt("v", T, Tt)
                if c == 0:
                    wt("v", Xr)
                V.inc()
                em("v", lambda e, i=pbank[:, 0:128], o=fpv(XRL[b % 2] + c * 128, 128):
                   e.tensor_tensor(o, i, o, ALU.add).then_inc(V.h, 1))
                V_tc[(b, c)] = V.n
            V_tadd[b] = V.n
            ST_st[b] = stdma(fpv(XRL[b % 2], 1024), y[b * 128:(b + 1) * 128, :],
                             wait=(V, V_tadd[b]))
        for s in STP:
            wt("y", s, s.n)

        @block.sync
        def _(eng):
            for f in streams["y"]:
                f(eng)

        @block.gpsimd
        def _(eng):
            for f in streams["g"]:
                f(eng)

        @block.tensor
        def _(eng):
            for f in streams["t"]:
                f(eng)

        @block.vector
        def _(eng):
            for f in streams["v"]:
                f(eng)

        @block.scalar
        def _(eng):
            for f in streams["s"]:
                f(eng)

    return nc


def _esel8():
    s = np.zeros((8, 1024), np.float32)
    for e in range(8):
        s[e, e * 128:(e + 1) * 128] = 1.0
    return s.astype(BF)


def _consts():
    masks = np.zeros((5, 128, 512), np.float32)
    for m in range(5):
        kk = np.arange(128)[:, None]
        qq = np.arange(512)[None, :]
        masks[m] = (np.abs(128 * m + kk - qq - 4) <= 3).astype(np.float32)
    return {
        "masks": masks.astype(BF),
        "eye_b": np.eye(128).astype(BF),
        "eye_f": np.eye(128, dtype=np.float32),
        "ones_cb": np.ones((128, 1)).astype(BF),
        "ones_rb": np.ones((1, 128)).astype(BF),
        "ones_rf": np.ones((1, 128), np.float32),
        "pat864": np.tile(np.eye(8, dtype=np.float32), (1, 8)).reshape(8, 8, 8).transpose(0, 2, 1).reshape(8, 64),
        "grp648": np.repeat(np.eye(8, dtype=np.float32), 8, axis=0),
        "grp864": np.repeat(np.eye(8, dtype=np.float32), 8, axis=0).T.copy(),
        "eyee": np.eye(8, dtype=np.float32),
        "tile648": np.tile(np.eye(8, dtype=np.float32), (8, 1)),
        "esel8": _esel8(),
    }


def kernel(L0_attn_x, L0_attn_w_qkv, L0_attn_proj, L0_attn_w_out, L0_attn_b_out,
           L0_ffn_Wg, L0_ffn_W1, L0_ffn_W2):
    x = np.ascontiguousarray(np.asarray(L0_attn_x, dtype=np.float32)).reshape(16384, 1024)
    consts = _consts()
    pat = np.zeros((8, 64), np.float32)
    for r in range(8):
        pat[:, r * 8:(r + 1) * 8] = np.eye(8)
    consts["pat864"] = pat
    base = dict(
        wqkv=np.ascontiguousarray(L0_attn_w_qkv, dtype=np.float32),
        proj=np.ascontiguousarray(L0_attn_proj, dtype=np.float32),
        wout=np.ascontiguousarray(L0_attn_w_out, dtype=np.float32),
        boutT=np.ascontiguousarray(np.asarray(L0_attn_b_out, dtype=np.float32)
                                   .reshape(8, 128).T),
        wg=np.ascontiguousarray(L0_ffn_Wg, dtype=np.float32),
        w1=np.ascontiguousarray(L0_ffn_W1, dtype=np.float32),
        w2=np.ascontiguousarray(L0_ffn_W2, dtype=np.float32),
        **consts,
    )
    in_maps = []
    for c in range(NC):
        start = c * TOK
        seqlo = 0 if c < 4 else 8192
        seqhi = seqlo + 8192
        m = dict(base)
        m["xm"] = x[start:start + TOK]
        m["xhl"] = (x[start - 3:start] if start > seqlo
                    else np.zeros((3, 1024), np.float32))
        m["xhr"] = (x[start + TOK:start + TOK + 3] if start + TOK < seqhi
                    else np.zeros((3, 1024), np.float32))
        hm = np.ones((64, 6), np.float32)
        if start == seqlo:
            hm[:, 0:3] = 0.0
        if start + TOK == seqhi:
            hm[:, 3:6] = 0.0
        m["hm"] = hm
        so = np.zeros((64, 8), np.float32)
        so[c * 8:(c + 1) * 8, :] = np.eye(8)
        m["selown"] = so
        in_maps.append(m)
    nc = build_nc()
    res = run_bass_kernel_spmd(nc, in_maps, core_ids=list(range(NC)))
    out = np.concatenate([res.results[c]["y"] for c in range(NC)], axis=0)
    if DEBUG_DUMPS:
        import pickle
        dd = {k: np.stack([np.asarray(res.results[c][k]).astype(np.float32)
                           for c in range(NC)])
              for k in res.results[0] if k.startswith("dbg_")}
        with open("/root/problem/scratch/dev_dumps.pkl", "wb") as f:
            pickle.dump(dd, f)
    return out.reshape(2, 8192, 1024).astype(np.float32)


# version 15
# speedup vs baseline: 1.0402x; 1.0316x over previous
import numpy as np
import ml_dtypes
from contextlib import ExitStack

import concourse.bass as bass
import concourse.mybir as mybir
from concourse.bass_utils import run_bass_kernel_spmd

F32 = mybir.dt.float32
BF16 = mybir.dt.bfloat16
AF = mybir.ActivationFunctionType
ALU = mybir.AluOpType
AX = mybir.AxisListType
BF = ml_dtypes.bfloat16

TOK = 2048
PW = 2176
OW0 = 4
D = 1024
H = 8
DH = 128
FEAT = 64
E = 8
FF = 2048
NC = 8
BISECT = 32
CAP = 2048

XN = [c * PW for c in range(8)]
MASK0 = 17408
EYEB = MASK0 + 2560
ONES128B = EYEB + 128
ONES1B = ONES128B + 16
PROJB = ONES1B + 128
SWG = PROJB + 64
ESEL8 = SWG + 64
BFDUM = ESEL8 + 1024
RA = BFDUM + 2048
WHEAD = RA
WQKP = RA + 3072
TRB = RA + 4096
VTM = RA + 4352
SST = RA + 6528
QPHI = RA + 9088
KPHI = RA + 11264
OALL = RA + 13440
A_END = OALL + 16384
W1B = RA
W2B = RA + 8192
ASB = RA + 16384
GBE = RA + 24576
B_END = GBE + 2048
NBF = max(A_END, B_END)

EYEF = 0
ONES1F = 128
BOUTT = 256
HM = 264
SELOWN = 280
PAT864 = 288
GRP648 = 352
GRP864 = 360
EYEE = 424
TILE648 = 432
STATS = 440
SMALL = 584
LO8, HI8, TH8, CNT64, ISGE, ISLT, TMP8, TH64 = (SMALL + i for i in range(8))
EPSC = SMALL + 8
CNT8 = SMALL + 9
PEM = 600
MPR = PEM + 2048
RMX = MPR + 2048
BIG = RMX + 2048
OACC = [BIG + c * 2048 for c in range(8)]
XT = [BIG, BIG + 1024]
SQD = BIG + 2048
AGC = BIG + 3072
SCR = BIG + 5120
LGT = BIG + 7168
SCA = BIG + 9216
SCB = BIG + 9728
ZSC = BIG + 10240
ZBS = BIG + 10752
D8S = BIG + 11264
BIGEND = BIG + 16384
XRL = [BIGEND, BIGEND + 1024]
GSC = BIGEND + 2048
NFP = BIGEND + 2560

GELU_BF = True
DEBUG_DUMPS = False
STOP_AFTER = None
HEAD_STAGE = None
GLVL = 9
SERIAL = 0


class SemC:
    def __init__(self, h):
        self.h = h
        self.n = 0

    def inc(self, k=1):
        self.n += k
        return self.n


def build_nc():
    nc = bass.Bass("TRN2", target_bir_lowering=False, debug=False,
                   detect_race_conditions=False)
    dt = nc.dram_tensor
    xm = dt("xm", [TOK, D], F32, kind="ExternalInput").ap()
    xhl = dt("xhl", [3, D], F32, kind="ExternalInput").ap()
    xhr = dt("xhr", [3, D], F32, kind="ExternalInput").ap()
    wqkv = dt("wqkv", [D, 3 * D], F32, kind="ExternalInput").ap()
    proj = dt("proj", [DH, FEAT], F32, kind="ExternalInput").ap()
    wout = dt("wout", [D, D], F32, kind="ExternalInput").ap()
    boutT = dt("boutT", [128, 8], F32, kind="ExternalInput").ap()
    wg = dt("wg", [D, E], F32, kind="ExternalInput").ap()
    w1 = dt("w1", [E, D, FF], F32, kind="ExternalInput").ap()
    w2 = dt("w2", [E, FF, D], F32, kind="ExternalInput").ap()
    masks = dt("masks", [5, 128, 512], BF16, kind="ExternalInput").ap()
    eye_b = dt("eye_b", [128, 128], BF16, kind="ExternalInput").ap()
    eye_f = dt("eye_f", [128, 128], F32, kind="ExternalInput").ap()
    ones_cb = dt("ones_cb", [128, 1], BF16, kind="ExternalInput").ap()
    ones_rb = dt("ones_rb", [1, 128], BF16, kind="ExternalInput").ap()
    ones_rf = dt("ones_rf", [1, 128], F32, kind="ExternalInput").ap()
    hm = dt("hm", [64, 6], F32, kind="ExternalInput").ap()
    selown = dt("selown", [64, 8], F32, kind="ExternalInput").ap()
    pat864 = dt("pat864", [8, 64], F32, kind="ExternalInput").ap()
    grp648 = dt("grp648", [64, 8], F32, kind="ExternalInput").ap()
    grp864 = dt("grp864", [8, 64], F32, kind="ExternalInput").ap()
    eyee = dt("eyee", [8, 8], F32, kind="ExternalInput").ap()
    tile648 = dt("tile648", [64, 8], F32, kind="ExternalInput").ap()
    esel8 = dt("esel8", [8, 1024], BF16, kind="ExternalInput").ap()
    y = dt("y", [TOK, D], F32, kind="ExternalOutput").ap()
    if DEBUG_DUMPS:
        dbg_pem = dt("dbg_pem", [8, TOK], F32, kind="ExternalOutput").ap()
        dbg_mpr = dt("dbg_mpr", [8, TOK], F32, kind="ExternalOutput").ap()
        dbg_rmx = dt("dbg_rmx", [1, TOK], F32, kind="ExternalOutput").ap()
        dbg_lo = dt("dbg_lo", [8, 1], F32, kind="ExternalOutput").ap()
        dbg_attn = dt("dbg_attn", [128, 8 * TOK], F32, kind="ExternalOutput").ap()
        dbg_oall = dt("dbg_oall", [128, 8 * TOK], BF16, kind="ExternalOutput").ap()
        dbg_xn = dt("dbg_xn", [128, 8 * PW], BF16, kind="ExternalOutput").ap()
        dbg_qphi = dt("dbg_qphi", [64, 2054], BF16, kind="ExternalOutput").ap()
        dbg_st = dt("dbg_st", [128, 144], F32, kind="ExternalOutput").ap()
        dbg_kphi = dt("dbg_kphi", [64, 2054], BF16, kind="ExternalOutput").ap()

    cc_in = dt("cc_in", [8, TOK], F32, kind="Internal").ap()
    cc_out = dt("cc_out", [64, TOK], F32, kind="Internal").ap()

    with ExitStack() as ctx:
        bfa = ctx.enter_context(nc.sbuf_tensor("bfa", [128, NBF], BF16))
        fpa = ctx.enter_context(nc.sbuf_tensor("fpa", [128, NFP], F32))
        pb = [ctx.enter_context(nc.psum_tensor(f"pb{i}", [128, 512], F32))
              for i in range(8)]
        YP = [SemC(ctx.enter_context(nc.semaphore(f"Y{i}"))) for i in range(4)]
        GP = [SemC(ctx.enter_context(nc.semaphore(f"Gq{i}"))) for i in range(4)]
        STP = [SemC(ctx.enter_context(nc.semaphore(f"SQ{i}"))) for i in range(2)]
        ydma_i = [0]
        gdma_i = [0]
        stdma_i = [0]
        CC = SemC(ctx.enter_context(nc.semaphore("CC")))
        T = SemC(ctx.enter_context(nc.semaphore("T")))
        V = SemC(ctx.enter_context(nc.semaphore("V")))
        S = SemC(ctx.enter_context(nc.semaphore("S")))
        G = SemC(ctx.enter_context(nc.semaphore("G")))
        block = ctx.enter_context(nc.Block())

        streams = {"t": [], "v": [], "s": [], "g": [], "y": []}

        def em(eng, f):
            streams[eng].append(f)

        def barrier():
            if not SERIAL:
                return
            sems = [T, V, S, G] + YP + GP + STP + [CC]
            for eng in ("t", "v", "s", "g", "y"):
                for sm in sems:
                    if sm.n > 0:
                        wt(eng, sm, sm.n)

        def wt(eng, sem, val=None):
            if val is None:
                sem, val = sem
            if val > 0:
                em(eng, lambda e, h=sem.h, v=val: e.wait_ge(h, v))

        def vsync():
            em("v", lambda e, v=V.n: e.wait_ge(V.h, v))

        def wtall(eng, snap):
            for sem, val in snap:
                wt(eng, sem, val)

        def bfv(col, w, p0=0, p1=128):
            return bfa[p0:p1, col:col + w]

        def fpv(col, w, p0=0, p1=128):
            return fpa[p0:p1, col:col + w]

        def _pool_dma(stream, pool, idx, src, dst_view, wait):
            sem = pool[idx[0] % len(pool)]
            idx[0] += 1
            if wait is not None:
                wt(stream, wait)
            if sem.n > 0:
                wt(stream, sem, sem.n)
            n = sem.inc(16)
            em(stream, lambda e, s=src, d=dst_view, h=sem.h:
               e.dma_start(out=d, in_=s).then_inc(h, 16))
            return (sem, n)

        def ydma(src, dst_view, wait=None):
            return _pool_dma("y", YP, ydma_i, src, dst_view, wait)

        def gdma(src, dst_view, wait=None):
            return _pool_dma("g", GP, gdma_i, src, dst_view, wait)

        def stdma(src_view, dst, wait=None):
            return _pool_dma("y", STP, stdma_i, src_view, dst, wait)

        ydma(eye_f, fpv(EYEF, 128))
        ydma(ones_rf, fpa[0:1, ONES1F:ONES1F + 128])
        ydma(boutT, fpv(BOUTT, 8))
        ydma(hm, fpa[0:64, HM:HM + 6])
        ydma(selown, fpa[0:64, SELOWN:SELOWN + 8])
        ydma(pat864, fpa[0:8, PAT864:PAT864 + 64])
        ydma(grp648, fpa[0:64, GRP648:GRP648 + 8])
        ydma(grp864, fpa[0:8, GRP864:GRP864 + 64])
        ydma(eyee, fpa[0:8, EYEE:EYEE + 8])
        ydma(tile648, fpa[0:64, TILE648:TILE648 + 8])
        ydma(esel8, bfa[0:8, ESEL8:ESEL8 + 1024])
        ydma(masks.rearrange("m p f -> p m f"),
             bfv(MASK0, 2560).rearrange("p (m f) -> p m f", m=5))
        ydma(eye_b, bfv(EYEB, 128))
        ydma(ones_cb, bfv(ONES128B, 1))
        ydma(ones_rb, bfa[0:1, ONES1B:ONES1B + 128])
        X_consts = [(s, s.n) for s in YP if s.n > 0]

        W_proj = gdma(proj, bfv(PROJB, 64))
        W_wg = gdma(wg.rearrange("(c p) e -> p c e", p=128),
                    bfv(SWG, 64).rearrange("p (c e) -> p c e", c=8))
        def whead_dma(h, wait_t):
            toks = []
            for i, base in enumerate((0, D, 2 * D)):
                toks.append(gdma(
                    wqkv[:, base + h * DH:base + (h + 1) * DH]
                    .rearrange("(c p) f -> p c f", p=128),
                    bfv(WHEAD + i * 1024, 1024).rearrange("p (c f) -> p c f", c=8),
                    wait=(T, wait_t) if i == 0 and wait_t else None))
            return toks
        W_whead = [None] * (H + 1)
        W_whead[0] = whead_dma(0, 0)
        def gmemset(view):
            G.inc(1)
            em("g", lambda e, d=view: e.memset(d, 0.0).then_inc(G.h, 1))
        gmemset(bfa[0:64, KPHI + 2056:KPHI + PW])
        for c in range(8):
            gmemset(bfa[:, XN[c]:XN[c] + 1])
            gmemset(bfa[:, XN[c] + 2055:XN[c] + PW])
        G_zeros = G.n

        SWOUT_LO = RA
        SWOUT_HI = RA + 4352
        SWOUT_COL = [SWOUT_LO + kc * 1024 if kc < 4 else SWOUT_HI + (kc - 4) * 1024
                     for kc in range(8)]
        W_swout = [None, None]

        V_xn = [0] * 18
        S_trev = [[0] * 8 for _ in range(18)]
        T_tr = [[0] * 8 for _ in range(18)]
        X_xt = [0] * 18

        def tile_rows(t):
            return 3 if t >= 16 else 128

        def tile_dst_col(t):
            if t < 16:
                return OW0 + 128 * t
            return 1 if t == 16 else 2052

        V.inc()
        em("v", lambda e, o=fpv(EPSC, 1): e.memset(o, 1e-5).then_inc(V.h, 1))
        for t in range(18):
            src = xm[t * 128:(t + 1) * 128, :] if t < 16 else (xhl if t == 16 else xhr)
            r = tile_rows(t)
            X_xt[t] = ydma(src, fpv(XT[t % 2], 1024, 0, r),
                           wait=(V, V_xn[t - 2]) if t >= 2 else None)
            xt = fpv(XT[t % 2], 1024, 0, r)
            stc = STATS + 8 * t
            wt("s", X_xt[t])
            S.inc()
            em("s", lambda e, xv=xt, o=fpv(SQD, 1024, 0, r), a=fpv(stc + 1, 1, 0, r):
               e.activation(o, xv, AF.Square, accum_out=a).then_inc(S.h, 1))
            S.inc()
            em("s", lambda e, xv=xt, o=fpv(SQD, 1024, 0, r), a=fpv(stc + 0, 1, 0, r):
               e.activation(o, xv, AF.Copy, accum_out=a).then_inc(S.h, 1))
            S_sq = S.n
            wt("v", S, S_sq)
            for fdef in (
                lambda e, i=fpv(stc, 1, 0, r), o=fpv(stc + 2, 1, 0, r):
                    e.tensor_scalar_mul(o, i, -1.0 / D),
                lambda e, a=fpv(stc, 1, 0, r), b=fpv(stc + 2, 1, 0, r),
                       o=fpv(stc + 3, 1, 0, r):
                    e.tensor_mul(o, a, b),
                lambda e, i=fpv(stc + 1, 1, 0, r), o=fpv(stc + 4, 1, 0, r):
                    e.tensor_scalar_mul(o, i, 1.0 / D),
                lambda e, a=fpv(stc + 3, 1, 0, r), b=fpv(stc + 4, 1, 0, r),
                       o=fpv(stc + 5, 1, 0, r):
                    e.scalar_tensor_tensor(o, a, 1.0 / D, b, ALU.mult, ALU.add),
            ):
                vsync()
                V.inc()
                em("v", lambda e, f=fdef: f(e).then_inc(V.h, 1))
            V_var = V.n
            wt("s", V, V_var)
            S.inc()
            em("s", lambda e, i=fpv(stc + 5, 1, 0, r), o=fpv(stc + 6, 1, 0, r),
                   b=fpv(EPSC, 1, 0, r):
               e.activation(o, i, AF.Sqrt, bias=b).then_inc(S.h, 1))
            S_rs = S.n
            wt("v", S, S_rs)
            V.inc()
            em("v", lambda e, i=fpv(stc + 6, 1, 0, r), o=fpv(stc + 7, 1, 0, r):
               e.reciprocal(o, i).then_inc(V.h, 1))
            vsync()
            xtm_col = SST + (t % 2) * 1024
            if t >= 2:
                wt("v", T, T_tr[t - 2][7])
            V.inc()
            em("v", lambda e, xv=xt, m=fpv(stc + 2, 1, 0, r), sdv=fpv(stc + 7, 1, 0, r),
                   o=bfv(xtm_col, 1024, 0, r):
               e.tensor_scalar(o, xv, m, sdv, ALU.add, ALU.mult).then_inc(V.h, 1))
            V_xn[t] = V.n
            dst = tile_dst_col(t)
            for c in range(8):
                pbank = pb[6 + ((t * 8 + c) % 2)]
                prev = (t * 8 + c) - 2
                if prev >= 0:
                    wt("t", S, S_trev[prev // 8][prev % 8])
                if c == 0:
                    wt("t", V, V_xn[t])
                    if t == 0:
                            wtall("t", X_consts)
                T.inc()
                if r == 128:
                    em("t", lambda e, l=bfv(xtm_col + c * 128, 128), rh=bfv(EYEB, 128),
                           o=pbank[:, 0:128]:
                       e.matmul(o, l, rh, start=True, stop=True).then_inc(T.h, 1))
                else:
                    em("t", lambda e, l=bfv(xtm_col + c * 128, 128, 0, 3),
                           rh=bfv(EYEB, 3, 0, 3), o=pbank[:, 0:3]:
                       e.matmul(o, l, rh, start=True, stop=True).then_inc(T.h, 1))
                T_tr[t][c] = T.n
                wt("s", T, T_tr[t][c])
                wd = 3 if r == 3 else 128
                S.inc()
                em("s", lambda e, i=pbank[:, 0:wd], o=bfv(XN[c] + dst, wd), wd=wd:
                   e.copy(o, i).then_inc(S.h, 1))
                S_trev[t][c] = S.n
        S_ln_done = S.n
        barrier()
        S_xn_main = S_trev[15][7]

        V_lgt = [0] * 4
        for ts in range(4):
            wt("t", S, S_ln_done)
            if ts == 0:
                wt("t", W_wg)
                wt("t", W_proj)
            if ts > 0:
                wt("t", V, V_lgt[ts - 1])
            for kc in range(8):
                T.inc()
                em("t", lambda e, l=bfv(SWG + kc * 8, 8), rh=bfv(XN[kc] + OW0 + ts * 512, 512),
                       o=pb[7][0:8, :], st=(kc == 0), sp=(kc == 7):
                   e.matmul(o, l, rh, start=st, stop=sp).then_inc(T.h, 1))
            wt("v", T, T.n)
            V.inc()
            em("v", lambda e, i=pb[7][0:8, :], o=fpv(LGT + ts * 512, 512, 0, 8):
               e.tensor_copy(o, i).then_inc(V.h, 1))
            V_lgt[ts] = V.n
        wt("y", V, V_lgt[3])
        CC.inc(16)
        em("y", lambda e: e.dma_start(out=cc_in, in_=fpv(LGT, 2048, 0, 8)).then_inc(CC.h, 16))
        wt("g", CC, 16)
        CC.inc(1)
        em("g", lambda e: e.collective_compute(
            "AllGather", ALU.bypass, replica_groups=[list(range(NC))],
            ins=[cc_in], outs=[cc_out]).then_inc(CC.h, 1))
        wt("y", CC, 17)
        CC.inc(16)
        em("y", lambda e: e.dma_start(out=fpv(AGC, 2048, 0, 64), in_=cc_out).then_inc(CC.h, 16))
        wt("s", CC, 33)
        S.inc()
        em("s", lambda e, v=fpv(AGC, 2048, 0, 64): e.activation(v, v, AF.Exp).then_inc(S.h, 1))
        S_exp = S.n
        V_sm = [0] * 4
        for ts in range(4):
            wt("t", S, S_exp)
            if ts > 0:
                wt("t", V, V_sm[ts - 1])
            T.inc()
            em("t", lambda e, l=fpa[0:64, GRP648:GRP648 + 8],
                   rh=fpv(AGC + ts * 512, 512, 0, 64), o=pb[7][0:8, :]:
               e.matmul(o, l, rh, start=True, stop=True).then_inc(T.h, 1))
            wt("v", T, T.n)
            V.inc()
            em("v", lambda e, i=pb[7][0:8, :], o=fpv(D8S, 512, 0, 8):
               e.reciprocal(o, i).then_inc(V.h, 1))
            Vr = V.n
            wt("t", V, Vr)
            T.inc()
            em("t", lambda e, l=fpa[0:8, GRP864:GRP864 + 64], rh=fpv(D8S, 512, 0, 8),
                   o=pb[7][0:64, :]:
               e.matmul(o, l, rh, start=True, stop=True).then_inc(T.h, 1))
            wt("v", T, T.n)
            V.inc()
            em("v", lambda e, p=fpv(AGC + ts * 512, 512, 0, 64), i=pb[7][0:64, :]:
               e.tensor_tensor(p, p, i, ALU.mult).then_inc(V.h, 1))
            V_sm[ts] = V.n
        V_probs = V.n
        barrier()
        for col, val in ((LO8, 0.0), (HI8, 1.0)):
            V.inc()
            em("v", lambda e, o=fpv(col, 1, 0, 8), vv=val: e.memset(o, vv).then_inc(V.h, 1))
        V_bis_prev = [V.n]
        T_bis_prev = [0]
        DO_HEADS = STOP_AFTER not in ("ln", "route")
        H_CUT = H
        if STOP_AFTER and STOP_AFTER.startswith("h"):
            H_CUT = int(STOP_AFTER[1:])
        DO_OUTPROJ = STOP_AFTER in (None, "attn", "bis") and DO_HEADS
        DO_BISECT = STOP_AFTER in (None, "bis")
        DO_FFN = STOP_AFTER is None or (STOP_AFTER or "").startswith("ffn")
        FFN_GR = 16
        FFN_LVL = 9
        if STOP_AFTER and STOP_AFTER.startswith("ffn"):
            parts = STOP_AFTER[3:].split("_")
            FFN_GR = int(parts[0])
            if len(parts) > 1:
                FFN_LVL = int(parts[1])

        bis_state = {"k": 0, "counted": 0}

        def emit_bisect_theta():
            V.inc()
            em("v", lambda e, o=fpv(TH8, 1, 0, 8), a=fpv(LO8, 1, 0, 8), b=fpv(HI8, 1, 0, 8):
               e.tensor_add(o, a, b).then_inc(V.h, 1))
            vsync()
            V.inc()
            em("v", lambda e, o=fpv(TH8, 1, 0, 8):
               e.tensor_scalar_mul(o, o, 0.5).then_inc(V.h, 1))
            bis_state["thV"] = V.n

        def emit_bisect_phaseA():
            if "thV" not in bis_state:
                emit_bisect_theta()
            Vth = bis_state.pop("thV")
            if "cntS" in bis_state:
                wt("t", S, bis_state["cntS"])
            wt("t", V, Vth)
            T.inc()
            em("t", lambda e, l=fpa[0:8, PAT864:PAT864 + 64], rh=fpv(TH8, 1, 0, 8),
                   o=pb[7][0:64, 0:1]:
               e.matmul(o, l, rh, start=True, stop=True).then_inc(T.h, 1))
            T_bis_prev[0] = T.n
            wt("s", T, T.n)
            S.inc()
            em("s", lambda e, i=pb[7][0:64, 0:1], o=fpv(TH64, 1, 0, 64):
               e.copy(o, i).then_inc(S.h, 1))
            Sc = S.n
            wt("v", S, Sc)
            V.inc()
            em("v", lambda e, i=fpv(AGC, 2048, 0, 64), th=fpv(TH64, 1, 0, 64),
                   o=bfv(BFDUM, 2048, 0, 64), a=fpv(CNT64, 1, 0, 64):
               e.tensor_scalar(o, i, th, 0.0, ALU.is_gt, ALU.add,
                               accum_out=a).then_inc(V.h, 1))
            bis_state["cmpV"] = V.n

        def emit_bisect_phaseB():
            wt("t", V, bis_state["cmpV"])
            T.inc()
            em("t", lambda e, l=fpa[0:64, TILE648:TILE648 + 8], rh=fpv(CNT64, 1, 0, 64),
                   o=pb[7][0:8, 4:5]:
               e.matmul(o, l, rh, start=True, stop=True).then_inc(T.h, 1))
            wt("s", T, T.n)
            S.inc()
            em("s", lambda e, i=pb[7][0:8, 4:5], o=fpv(CNT8, 1, 0, 8):
               e.copy(o, i).then_inc(S.h, 1))
            Sc2 = S.n
            bis_state["cntS"] = S.n
            wt("v", S, Sc2)
            for fdef in (
                lambda e: e.tensor_scalar(fpv(ISGE, 1, 0, 8), fpv(CNT8, 1, 0, 8),
                                          CAP - 0.5, None, ALU.is_gt),
                lambda e: e.tensor_sub(fpv(TMP8, 1, 0, 8), fpv(TH8, 1, 0, 8), fpv(LO8, 1, 0, 8)),
                lambda e: e.tensor_mul(fpv(TMP8, 1, 0, 8), fpv(TMP8, 1, 0, 8), fpv(ISGE, 1, 0, 8)),
                lambda e: e.tensor_add(fpv(LO8, 1, 0, 8), fpv(LO8, 1, 0, 8), fpv(TMP8, 1, 0, 8)),
                lambda e: e.tensor_scalar(fpv(ISLT, 1, 0, 8), fpv(ISGE, 1, 0, 8), -1.0, -1.0,
                                          ALU.add, ALU.mult),
                lambda e: e.tensor_sub(fpv(TMP8, 1, 0, 8), fpv(TH8, 1, 0, 8), fpv(HI8, 1, 0, 8)),
                lambda e: e.tensor_mul(fpv(TMP8, 1, 0, 8), fpv(TMP8, 1, 0, 8), fpv(ISLT, 1, 0, 8)),
                lambda e: e.tensor_add(fpv(HI8, 1, 0, 8), fpv(HI8, 1, 0, 8), fpv(TMP8, 1, 0, 8)),
            ):
                vsync()
                V.inc()
                em("v", lambda e, f=fdef: f(e).then_inc(V.h, 1))
            V_bis_prev[0] = V.n
            wt("v", T, T_bis_prev[0])
            emit_bisect_theta()

        KOFF = [0, 512, 1024, 1536, 2048]
        KWID = [512, 512, 512, 512, 8]
        bis_emitted = [0]
        prev_szb = [0]
        pend = [None]

        def flush_oevac():
            if pend[0] is None:
                return
            hh, gg, Vz, TO, pO = pend[0]
            pend[0] = None
            wt("t", V, Vz)
            if prev_szb[0]:
                wt("t", S, prev_szb[0])
            T.inc()
            em("t", lambda e, l=fpa[0:1, ONES1F:ONES1F + 128], rh=fpv(ZSC, 512, 0, 1),
                   o=pb[4][:, :]:
               e.matmul(o, l, rh, start=True, stop=True).then_inc(T.h, 1))
            Tzb = T.n
            wt("s", T, Tzb)
            if V_oev_last[0]:
                wt("s", V, V_oev_last[0])
            S.inc()
            em("s", lambda e, i=pb[4][:, :], o=fpv(ZBS, 512):
               e.copy(o, i).then_inc(S.h, 1))
            Szb = S.n
            prev_szb[0] = Szb
            wt("v", T, TO)
            wt("v", S, Szb)
            V.inc()
            em("v", lambda e, i=pO[:, :], z=fpv(ZBS, 512),
                   o=bfv(OALL + hh * 2048 + gg * 512, 512):
               e.tensor_tensor(o, i, z, ALU.mult).then_inc(V.h, 1))
            V_oev_prev[(hh, gg)] = V.n
            V_oev_last[0] = V.n

        V_oev_last = [0]
        S_vtm = [[0] * 17 for _ in range(H)]
        V_oall = [0] * H
        V_mask_prev = {}
        V_oev_prev = {}
        T_vlast = [0] * H
        T_head_last = [0] * H

        for h in range(H_CUT if DO_HEADS else 0):
            wtall("t", W_whead[h])
            wtall("t", X_consts)
            if h > 0:
                wt("t", V, V_oall[h - 1])
                wt("t", S, S.n)
            S_wqkp = 0
            Tp_prev = 0
            for kc in range(8):
                pt = pb[4 + (kc % 2)]
                po = pb[4 + ((kc + 1) % 2)]
                if S_wqkp:
                    wt("t", S, S_wqkp)
                S_tr = 0
                for qi in range(2):
                    if qi == 1:
                        wt("t", S, S_tr)
                    T.inc()
                    em("t", lambda e, l=bfv(WHEAD + qi * 1024 + kc * 128, 128),
                           rh=bfv(EYEB, 128), o=pt[:, 0:128]:
                       e.matmul(o, l, rh, start=True, stop=True).then_inc(T.h, 1))
                    Ttr = T.n
                    wt("s", T, Ttr)
                    if qi == 0 and Tp_prev:
                        wt("s", T, Tp_prev)
                    S.inc()
                    em("s", lambda e, i=pt[:, 0:128], o=bfv(TRB + qi * 128, 128):
                       e.copy(o, i).then_inc(S.h, 1))
                    S_tr = S.n
                wt("t", S, S_tr)
                for qi in range(2):
                    T.inc()
                    em("t", lambda e, l=bfv(TRB + qi * 128, 128), rh=bfv(PROJB, 64),
                           o=po[:, qi * 64:(qi + 1) * 64]:
                       e.matmul(o, l, rh, start=True, stop=True).then_inc(T.h, 1))
                Tp = T.n
                Tp_prev = Tp
                wt("s", T, Tp)
                S.inc()
                em("s", lambda e, i=po[:, 0:128], o=bfv(WQKP + kc * 128, 128):
                   e.copy(o, i).then_inc(S.h, 1))
                S_wqkp = S.n
            for b in range(0 if HEAD_STAGE == "pre" else 17):
                pbank = pb[b % 2]
                if b >= 2:
                    wt("t", S, S_vtm[h][b - 2])
                elif h > 0 and b < 2:
                    wt("t", S, S_vtm[h - 1][15 + b])
                if b == 0:
                    if h == 0:
                        wt("t", S, S_ln_done)
                        wt("t", G, G_zeros)
                for kc in range(8):
                    T.inc()
                    em("t", lambda e, l=bfv(XN[kc] + b * 128, 128),
                           rh=bfv(WHEAD + 2048 + kc * 128, 128),
                           o=pbank[:, 0:128], st=(kc == 0), sp=(kc == 7):
                       e.matmul(o, l, rh, start=st, stop=sp).then_inc(T.h, 1))
                T_vlast[h] = T.n
                wt("s", T, T.n)
                S.inc()
                em("s", lambda e, i=pbank[:, 0:128], o=bfv(VTM + b * 128, 128):
                   e.copy(o, i).then_inc(S.h, 1))
                S_vtm[h][b] = S.n
            if h + 1 < H:
                W_whead[h + 1] = whead_dma(h + 1, T_vlast[h])
            elif h == H - 1:
                pass
            S_phi = 0
            V_phi = 0
            qjobs = [(0, QPHI, OW0 + 512 * g, 512) for g in range(4)]
            kjobs = [(64, KPHI, o, 512) for o in (0, 512, 1024, 1536, 1664)]
            phi_jobs = [] if HEAD_STAGE in ("pre", "v") else qjobs + kjobs
            V_phi_list = [0] * len(phi_jobs)
            V_min_list = [0] * len(phi_jobs)
            S_relu_list = [0] * len(phi_jobs)
            for ji, (wcol, dstc, off, wd) in enumerate(phi_jobs):
                pbank = pb[2 + (ji % 2)]
                wt("t", S, S_wqkp)
                if ji >= 2:
                    wt("t", V, V_min_list[ji - 2])
                for kc in range(8):
                    T.inc()
                    em("t", lambda e, l=bfv(WQKP + kc * 128 + wcol, 64),
                           rh=bfv(XN[kc] + off, wd), o=pbank[0:64, 0:wd],
                           st=(kc == 0), sp=(kc == 7):
                       e.matmul(o, l, rh, start=st, stop=sp).then_inc(T.h, 1))
                Tph = T.n
                wt("v", T, Tph)
                V.inc()
                em("v", lambda e, i=pbank[0:64, 0:wd], o=fpv(SCA, wd, 0, 64):
                   e.tensor_scalar_max(o, i, 0.0).then_inc(V.h, 1))
                V.inc()
                em("v", lambda e, i=pbank[0:64, 0:wd], o=fpv(SCB, wd, 0, 64):
                   e.tensor_scalar_min(o, i, 0.0).then_inc(V.h, 1))
                V_min_list[ji] = V.n
                wt("s", V, V.n)
                S.inc()
                em("s", lambda e, o=fpv(SCB, wd, 0, 64):
                   e.activation(o, o, AF.Exp).then_inc(S.h, 1))
                S_phi = S.n
                wt("v", S, S_phi)
                V.inc()
                em("v", lambda e, a=fpv(SCA, wd, 0, 64), b=fpv(SCB, wd, 0, 64),
                       o=bfv(dstc + off, wd, 0, 64):
                   e.tensor_add(o, a, b).then_inc(V.h, 1))
                V_phi_list[ji] = V.n
                V_phi = V.n
            T_phi_last = T.n
            for (c0, w0, hmc) in (((1, 3, HM), (2052, 3, HM + 3))
                                  if HEAD_STAGE in (None, "phi") else ()):
                V.inc()
                em("v", lambda e, kv=bfv(KPHI + c0, w0, 0, 64),
                       m=fpv(hmc, w0, 0, 64):
                   e.tensor_tensor(kv, kv, m, ALU.mult).then_inc(V.h, 1))
            V_kphi = V.n
            for g in range(0 if HEAD_STAGE in ("pre", "v", "phi") else 4):
                Ts_tile = [0] * 5
                V_mask = [0] * 5
                for m in range(5):
                    pbank = pb[m % 3]
                    key = (h, g, m)
                    if m >= 3:
                        pkey = (h, g, m - 3)
                    elif g > 0:
                        pkey = (h, g - 1, m + 3 if m < 2 else 2)
                    else:
                        pkey = (h - 1, 3, m + 3 if m < 2 else 2)
                    if pkey in V_mask_prev:
                        wt("t", V, V_mask_prev[pkey])
                    if m == 0 and g == 0:
                        wt("t", V, V_kphi)
                        wt("t", S, S_vtm[h][16])
                    T.inc()
                    em("t", lambda e, l=bfv(KPHI + g * 512 + m * 128, 128, 0, 64),
                           rh=bfv(QPHI + OW0 + g * 512, 512, 0, 64), o=pbank[:, :]:
                       e.matmul(o, l, rh, start=True, stop=True).then_inc(T.h, 1))
                    Ts_tile[m] = T.n
                    wt("v", T, Ts_tile[m])
                    V.inc()
                    em("v", lambda e, i=pbank[:, :], mk=bfv(MASK0 + m * 512, 512),
                           o=bfv(SST + m * 512, 512):
                       e.tensor_tensor(o, i, mk, ALU.mult).then_inc(V.h, 1))
                    V_mask[m] = V.n
                    V_mask_prev[key] = V.n
                if DO_BISECT and bis_state["counted"] < bis_state["k"]:
                    emit_bisect_phaseB()
                    bis_state["counted"] += 1
                flush_oevac()
                wt("t", V, V_mask[4])
                for m in range(5):
                    T.inc()
                    em("t", lambda e, l=bfv(ONES128B, 1), rh=bfv(SST + m * 512, 512),
                           o=pb[3][0:1, :], st=(m == 0), sp=(m == 4):
                       e.matmul(o, l, rh, start=st, stop=sp).then_inc(T.h, 1))
                Tz = T.n
                wt("v", T, Tz)
                V.inc()
                em("v", lambda e, i=pb[3][0:1, :], o=fpv(ZSC, 512, 0, 1):
                   e.tensor_scalar_add(o, i, 1e-8).then_inc(V.h, 1))
                vsync()
                V.inc()
                em("v", lambda e, o=fpv(ZSC, 512, 0, 1): e.reciprocal(o, o).then_inc(V.h, 1))
                Vz = V.n
                pO = pb[5 + (g % 2)]
                okey = (h, g - 2) if g >= 2 else (h - 1, g + 2)
                if okey in V_oev_prev:
                    wt("t", V, V_oev_prev[okey])
                if g < 2:
                    wt("t", S, S_wqkp)
                for m in range(5):
                    T.inc()
                    em("t", lambda e, l=bfv(VTM + (g * 4 + m) * 128, 128),
                           rh=bfv(SST + m * 512, 512), o=pO[:, :],
                           st=(m == 0), sp=(m == 4):
                       e.matmul(o, l, rh, start=st, stop=sp).then_inc(T.h, 1))
                TO = T.n
                T_head_last[h] = T.n
                pend[0] = (h, g, Vz, TO, pO)
                if DO_BISECT and bis_state["counted"] < bis_state["k"]:
                    emit_bisect_phaseB()
                    bis_state["counted"] += 1
                if DO_BISECT and bis_state["k"] < BISECT:
                    emit_bisect_phaseA()
                    bis_state["k"] += 1
            flush_oevac()
            V_oall[h] = V.n if HEAD_STAGE is None else max(V.n, 1)
            barrier()
            if h == H - 1 and DO_OUTPROJ:
                for kc in range(4):
                    W_swout[0] = gdma(wout[kc * 128:(kc + 1) * 128, :],
                                      bfv(SWOUT_COL[kc], 1024),
                                      wait=(T, T_phi_last) if kc == 0 else None)
                for kc in range(4, 8):
                    W_swout[1] = gdma(wout[kc * 128:(kc + 1) * 128, :],
                                      bfv(SWOUT_COL[kc], 1024),
                                      wait=(T, T_head_last[h]) if kc == 4 else None)


        if DO_BISECT and bis_state["counted"] < bis_state["k"]:
            emit_bisect_phaseB()
            bis_state["counted"] += 1
        V_pem = [0] * 4
        for ts in range(4 if DO_BISECT else 0):
            wt("t", V, V_bis_prev[0])
            if ts > 0:
                wt("t", V, V_pem[ts - 1])
            T.inc()
            em("t", lambda e, l=fpa[0:64, SELOWN:SELOWN + 8],
                   rh=fpv(AGC + ts * 512, 512, 0, 64), o=pb[7][0:8, :]:
               e.matmul(o, l, rh, start=True, stop=True).then_inc(T.h, 1))
            wt("v", T, T.n)
            V.inc()
            em("v", lambda e, i=pb[7][0:8, :], o=fpv(PEM + ts * 512, 512, 0, 8):
               e.tensor_copy(o, i).then_inc(V.h, 1))
            V_pem[ts] = V.n
        if DO_BISECT:
            V.inc()
            em("v", lambda e, i=fpv(PEM, 2048, 0, 8), th=fpv(LO8, 1, 0, 8),
                   o=fpv(SCR, 2048, 0, 8):
               e.tensor_scalar(o, i, th, None, ALU.is_gt).then_inc(V.h, 1))
            V.inc()
            em("v", lambda e, s=fpv(SCR, 2048, 0, 8), p=fpv(PEM, 2048, 0, 8),
                   o=fpv(MPR, 2048, 0, 8):
               e.scalar_tensor_tensor(o, s, 2.0, p, ALU.mult, ALU.add).then_inc(V.h, 1))
        V_mpr = V.n
        V_rmx_done = 0
        for ts in range(4 if DO_BISECT else 0):
            for ee in range(8):
                pbank = pb[6 + (ee % 2)]
                wt("t", V, V_mpr if ee == 0 and ts == 0 else V.n)
                T.inc()
                em("t", lambda e, l=fpa[0:8, EYEE + ee:EYEE + ee + 1],
                       rh=fpv(MPR + ts * 512, 512, 0, 8), o=pbank[0:1, :]:
                   e.matmul(o, l, rh, start=True, stop=True).then_inc(T.h, 1))
                Tx = T.n
                wt("v", T, Tx)
                V.inc()
                if ee == 0:
                    em("v", lambda e, i=pbank[0:1, :], o=fpv(RMX + ts * 512, 512, 0, 1):
                       e.tensor_copy(o, i).then_inc(V.h, 1))
                else:
                    em("v", lambda e, i=pbank[0:1, :], o=fpv(RMX + ts * 512, 512, 0, 1):
                       e.tensor_tensor(o, o, i, ALU.max).then_inc(V.h, 1))
        V_rmx_done = V.n

        V_opj = {}
        V_opj_last = V.n
        T_opj_last = T.n
        for ts in range(4 if DO_OUTPROJ else 0):
            for c in range(8):
                pbank = pb[(ts * 8 + c) % 4]
                pkey = (ts * 8 + c) - 4
                if pkey >= 0:
                    wt("t", V, V_opj[pkey])
                if ts == 0 and c == 0:
                    wt("t", V, V_oall[H - 1])
                    wt("t", W_swout[0])
                for kc in range(8):
                    if ts == 0 and c == 0 and kc == 4:
                        wt("t", W_swout[1])
                    T.inc()
                    em("t", lambda e, l=bfv(SWOUT_COL[kc] + c * 128, 128),
                           rh=bfv(OALL + kc * 2048 + ts * 512, 512), o=pbank[:, :],
                           st=(kc == 0), sp=(kc == 7):
                       e.matmul(o, l, rh, start=st, stop=sp).then_inc(T.h, 1))
                Tp = T.n
                wt("v", T, Tp)
                V.inc()
                em("v", lambda e, i=pbank[:, :], b=fpv(BOUTT + c, 1),
                       o=fpv(OACC[c] + ts * 512, 512):
                   e.tensor_scalar(o, i, b, None, ALU.add).then_inc(V.h, 1))
                V_opj[ts * 8 + c] = V.n
        if DO_HEADS:
            T_opj_last = T.n
            V_opj_last = V.n
        dbg_toks = []
        if DEBUG_DUMPS:
            dbg_toks.append(ydma(fpv(BIG, 8 * TOK), dbg_attn, wait=(V, V_opj_last)))
            dbg_toks.append(ydma(bfv(OALL, 8 * TOK), dbg_oall))
            dbg_toks.append(ydma(bfv(0, 8 * PW), dbg_xn))
            dbg_toks.append(ydma(fpv(PEM, TOK, 0, 8), dbg_pem))
            dbg_toks.append(ydma(fpv(MPR, TOK, 0, 8), dbg_mpr))
            dbg_toks.append(ydma(fpv(RMX, TOK, 0, 1), dbg_rmx))
            dbg_toks.append(ydma(fpv(LO8, 1, 0, 8), dbg_lo))
            dbg_toks.append(ydma(bfv(QPHI + 1, 2054, 0, 64), dbg_qphi))
            dbg_toks.append(ydma(fpv(STATS, 144), dbg_st))
            dbg_toks.append(ydma(bfv(KPHI + 1, 2054, 0, 64), dbg_kphi))
            for tok in dbg_toks:
                wt("v", tok)
                wt("s", tok)

        def w1_dma(e_, fh, wait_t):
            return gdma(w1[e_, :, fh * 1024:(fh + 1) * 1024]
                        .rearrange("(kc p) f -> p kc f", p=128),
                        bfv(W1B, 8192).rearrange("p (kc f) -> p kc f", kc=8),
                        wait=(T, wait_t) if wait_t else None)

        def w2_dma(e_, fh, wait_t):
            return gdma(w2[e_, fh * 1024:(fh + 1) * 1024, :]
                        .rearrange("(kc p) f -> p kc f", p=128),
                        bfv(W2B, 8192).rearrange("p (kc f) -> p kc f", kc=8),
                        wait=(T, wait_t) if wait_t else None)

        granules = ([(e_, fh) for e_ in range(E) for fh in range(2)][:FFN_GR]
                    if DO_FFN else [])
        W_w1 = {}
        W_w2 = {}
        if DEBUG_DUMPS:
            for tok in dbg_toks:
                wt("g", tok)
        if DO_FFN:
            W_w1[granules[0]] = w1_dma(0, 0, T_opj_last)
            W_w2[granules[0]] = w2_dma(0, 0, T_opj_last)

        V_gem = 0
        if DO_BISECT:
            for ts in range(4):
                wt("t", V, V_rmx_done if ts == 0 else V_gem)
                T.inc()
                em("t", lambda e, l=fpa[0:1, ONES1F:ONES1F + 8],
                       rh=fpv(RMX + ts * 512, 512, 0, 1), o=pb[7][0:8, :]:
                   e.matmul(o, l, rh, start=True, stop=True).then_inc(T.h, 1))
                wt("s", T, T.n)
                S.inc()
                em("s", lambda e, i=pb[7][0:8, :], o=fpv(GSC, 512, 0, 8):
                   e.copy(o, i).then_inc(S.h, 1))
                wt("v", S, S.n)
                V.inc()
                em("v", lambda e, a=fpv(GSC, 512, 0, 8), m=fpv(MPR + ts * 512, 512, 0, 8):
                   e.tensor_tensor(a, m, a, ALU.is_ge).then_inc(V.h, 1))
                V.inc()
                em("v", lambda e, a=fpv(GSC, 512, 0, 8), p=fpv(PEM + ts * 512, 512, 0, 8),
                       o=bfv(BFDUM + ts * 512, 512, 0, 8):
                   e.tensor_tensor(o, a, p, ALU.mult).then_inc(V.h, 1))
                V_gem = V.n

        V_gbe = [0] * E
        v_gp_prev = [V_gem]

        def emit_gate_prep(e_):
            v_prev = v_gp_prev
            for ts in range(4):
                wt("t", V, max(V_gem, v_prev[0]))
                pbank = pb[6 + (ts % 2)]
                T.inc()
                em("t", lambda e, l=bfa[0:8, ESEL8 + e_ * 128:ESEL8 + (e_ + 1) * 128],
                       rh=bfv(BFDUM + ts * 512, 512, 0, 8), o=pbank[:, :]:
                   e.matmul(o, l, rh, start=True, stop=True).then_inc(T.h, 1))
                Tb = T.n
                wt("v", T, Tb)
                V.inc()
                em("v", lambda e, i=pbank[:, :], o=bfv(GBE + ts * 512, 512):
                   e.tensor_copy(o, i).then_inc(V.h, 1))
                v_prev[0] = V.n
            V_gbe[e_] = V.n

        T_u_last = 0
        T_f2_last = T_opj_last
        V_f2 = {}
        S_gelu = {}
        V_gate = {}
        for gi, (e_, fh) in enumerate(granules):
            if fh == 0 and FFN_LVL >= 3:
                emit_gate_prep(e_)
            for th in range(2):
                for ts in range(2):
                    for f in range(8):
                        pbank = pb[f % 2]
                        idx = (gi, th, ts, f)
                        pidx = (gi, th, ts, f - 2)
                        if f >= 2 and (gi, th, ts, f - 2) in S_gelu:
                            wt("t", S, S_gelu[(gi, th, ts, f - 2)])
                        elif ts == 0 and f < 2:
                            pk = (gi, th - 1, 1, 6 + f)
                            if pk in S_gelu:
                                wt("t", S, S_gelu[pk])
                            pk2 = (gi - 1, 1, 1, 6 + f)
                            if th == 0 and pk2 in S_gelu:
                                wt("t", S, S_gelu[pk2])
                        elif ts == 1 and f < 2:
                            pk = (gi, th, 0, 6 + f)
                            if pk in S_gelu:
                                wt("t", S, S_gelu[pk])
                        if th == 0 and ts == 0 and f == 0:
                            wt("t", W_w1[(e_, fh)])
                            if gi == 0:
                                wt("t", V, V_opj_last)
                        for kc in range(8):
                            T.inc()
                            em("t", lambda e, l=bfv(W1B + kc * 1024 + f * 128, 128),
                                   rh=bfv(XN[kc] + OW0 + th * 1024 + ts * 512, 512),
                                   o=pbank[:, :], st=(kc == 0), sp=(kc == 7):
                               e.matmul(o, l, rh, start=st, stop=sp).then_inc(T.h, 1))
                        T_u_last = T.n
                        Tu = T.n
                        if FFN_LVL < 2:
                            S_gelu[idx] = 0
                            V_gate[(gi, th, ts, f)] = 0
                            continue
                        if ts == 0 and f == 0:
                            wt("s", T, T_f2_last)
                        wt("s", T, Tu)
                        pkg = (gi, th - 1, ts, f) if th == 1 else (gi - 1, 1, ts, f)
                        if pkg in V_gate:
                            wt("s", V, V_gate[pkg])
                        S.inc()
                        em("s", lambda e, i=pbank[:, :],
                               o=bfv(ASB + f * 1024 + ts * 512, 512):
                           e.activation(o, i, AF.Gelu).then_inc(S.h, 1))
                        S_gelu[idx] = S.n
                        S_gelu2 = S.n
                        if FFN_LVL < 4:
                            V_gate[(gi, th, ts, f)] = 0
                            continue
                        wt("v", S, S_gelu2)
                        if th == 0 and ts == 0 and f == 0:
                            wt("v", V, V_gbe[e_])
                        V.inc()
                        em("v", lambda e, a=bfv(ASB + f * 1024 + ts * 512, 512),
                               g=bfv(GBE + th * 1024 + ts * 512, 512):
                           e.tensor_tensor(a, a, g, ALU.mult).then_inc(V.h, 1))
                        V_gate[(gi, th, ts, f)] = V.n
                if th == 1 and gi + 1 < len(granules):
                    W_w1[granules[gi + 1]] = w1_dma(*granules[gi + 1], T_u_last)
                for ts in range(2 if FFN_LVL >= 5 else 0):
                    for c in range(8):
                        pbank = pb[2 + ((ts * 8 + c) % 4)]
                        pkey = (gi, th, ts * 8 + c - 4)
                        if ts * 8 + c - 4 >= 0 and pkey in V_f2:
                            wt("t", V, V_f2[pkey])
                        else:
                            pk = (gi, th - 1, ts * 8 + c + 12) if th == 1 else (gi - 1, 1, ts * 8 + c + 12)
                            if pk in V_f2:
                                wt("t", V, V_f2[pk])
                        if th == 0 and ts == 0 and c == 0:
                            wt("t", W_w2[(e_, fh)])
                        if c == 0:
                            wt("t", V, V_gate[(gi, th, ts, 7)])
                        for kc in range(8):
                            T.inc()
                            em("t", lambda e, l=bfv(W2B + kc * 1024 + c * 128, 128),
                                   rh=bfv(ASB + kc * 1024 + ts * 512, 512),
                                   o=pbank[:, :], st=(kc == 0), sp=(kc == 7):
                               e.matmul(o, l, rh, start=st, stop=sp).then_inc(T.h, 1))
                        T_f2_last = T.n
                        Tf = T.n
                        wt("v", T, Tf)
                        V.inc()
                        em("v", lambda e, i=pbank[:, :],
                               o=fpv(OACC[c] + th * 1024 + ts * 512, 512):
                           e.tensor_tensor(o, i, o, ALU.add).then_inc(V.h, 1))
                        V_f2[(gi, th, ts * 8 + c)] = V.n
                if th == 1 and gi + 1 < len(granules):
                    W_w2[granules[gi + 1]] = w2_dma(*granules[gi + 1], T_f2_last)
            barrier()
        V_ffn_done = V.n

        V_tadd = [0] * 16
        ST_st = [None] * 16
        V_tc = {}
        for b in range(16):
            w = (V, V_tadd[b - 2]) if b >= 2 else None
            Xr = ydma(xm[b * 128:(b + 1) * 128, :], fpv(XRL[b % 2], 1024), wait=w)
            if b >= 2:
                wt("y", ST_st[b - 2])
            for c in range(8):
                pbank = pb[6 + (c % 2)]
                if b == 0 and c == 0:
                    wt("t", V, V_ffn_done)
                pk = (b, c - 2) if c >= 2 else (b - 1, c + 6)
                if pk in V_tc:
                    wt("t", V, V_tc[pk])
                T.inc()
                em("t", lambda e, l=fpv(OACC[c] + b * 128, 128), rh=fpv(EYEF, 128),
                       o=pbank[:, 0:128]:
                   e.matmul(o, l, rh, start=True, stop=True).then_inc(T.h, 1))
                Tt = T.n
                wt("v", T, Tt)
                if c == 0:
                    wt("v", Xr)
                V.inc()
                em("v", lambda e, i=pbank[:, 0:128], o=fpv(XRL[b % 2] + c * 128, 128):
                   e.tensor_tensor(o, i, o, ALU.add).then_inc(V.h, 1))
                V_tc[(b, c)] = V.n
            V_tadd[b] = V.n
            ST_st[b] = stdma(fpv(XRL[b % 2], 1024), y[b * 128:(b + 1) * 128, :],
                             wait=(V, V_tadd[b]))
        for s in STP:
            wt("y", s, s.n)

        @block.sync
        def _(eng):
            for f in streams["y"]:
                f(eng)

        @block.gpsimd
        def _(eng):
            for f in streams["g"]:
                f(eng)

        @block.tensor
        def _(eng):
            for f in streams["t"]:
                f(eng)

        @block.vector
        def _(eng):
            for f in streams["v"]:
                f(eng)

        @block.scalar
        def _(eng):
            for f in streams["s"]:
                f(eng)

    return nc


def _esel8():
    s = np.zeros((8, 1024), np.float32)
    for e in range(8):
        s[e, e * 128:(e + 1) * 128] = 1.0
    return s.astype(BF)


def _consts():
    masks = np.zeros((5, 128, 512), np.float32)
    for m in range(5):
        kk = np.arange(128)[:, None]
        qq = np.arange(512)[None, :]
        masks[m] = (np.abs(128 * m + kk - qq - 4) <= 3).astype(np.float32)
    return {
        "masks": masks.astype(BF),
        "eye_b": np.eye(128).astype(BF),
        "eye_f": np.eye(128, dtype=np.float32),
        "ones_cb": np.ones((128, 1)).astype(BF),
        "ones_rb": np.ones((1, 128)).astype(BF),
        "ones_rf": np.ones((1, 128), np.float32),
        "pat864": np.tile(np.eye(8, dtype=np.float32), (1, 8)).reshape(8, 8, 8).transpose(0, 2, 1).reshape(8, 64),
        "grp648": np.repeat(np.eye(8, dtype=np.float32), 8, axis=0),
        "grp864": np.repeat(np.eye(8, dtype=np.float32), 8, axis=0).T.copy(),
        "eyee": np.eye(8, dtype=np.float32),
        "tile648": np.tile(np.eye(8, dtype=np.float32), (8, 1)),
        "esel8": _esel8(),
    }


def kernel(L0_attn_x, L0_attn_w_qkv, L0_attn_proj, L0_attn_w_out, L0_attn_b_out,
           L0_ffn_Wg, L0_ffn_W1, L0_ffn_W2):
    x = np.ascontiguousarray(np.asarray(L0_attn_x, dtype=np.float32)).reshape(16384, 1024)
    consts = _consts()
    pat = np.zeros((8, 64), np.float32)
    for r in range(8):
        pat[:, r * 8:(r + 1) * 8] = np.eye(8)
    consts["pat864"] = pat
    base = dict(
        wqkv=np.ascontiguousarray(L0_attn_w_qkv, dtype=np.float32),
        proj=np.ascontiguousarray(L0_attn_proj, dtype=np.float32),
        wout=np.ascontiguousarray(L0_attn_w_out, dtype=np.float32),
        boutT=np.ascontiguousarray(np.asarray(L0_attn_b_out, dtype=np.float32)
                                   .reshape(8, 128).T),
        wg=np.ascontiguousarray(L0_ffn_Wg, dtype=np.float32),
        w1=np.ascontiguousarray(L0_ffn_W1, dtype=np.float32),
        w2=np.ascontiguousarray(L0_ffn_W2, dtype=np.float32),
        **consts,
    )
    in_maps = []
    for c in range(NC):
        start = c * TOK
        seqlo = 0 if c < 4 else 8192
        seqhi = seqlo + 8192
        m = dict(base)
        m["xm"] = x[start:start + TOK]
        m["xhl"] = (x[start - 3:start] if start > seqlo
                    else np.zeros((3, 1024), np.float32))
        m["xhr"] = (x[start + TOK:start + TOK + 3] if start + TOK < seqhi
                    else np.zeros((3, 1024), np.float32))
        hm = np.ones((64, 6), np.float32)
        if start == seqlo:
            hm[:, 0:3] = 0.0
        if start + TOK == seqhi:
            hm[:, 3:6] = 0.0
        m["hm"] = hm
        so = np.zeros((64, 8), np.float32)
        so[c * 8:(c + 1) * 8, :] = np.eye(8)
        m["selown"] = so
        in_maps.append(m)
    nc = build_nc()
    res = run_bass_kernel_spmd(nc, in_maps, core_ids=list(range(NC)))
    out = np.concatenate([res.results[c]["y"] for c in range(NC)], axis=0)
    if DEBUG_DUMPS:
        import pickle
        dd = {k: np.stack([np.asarray(res.results[c][k]).astype(np.float32)
                           for c in range(NC)])
              for k in res.results[0] if k.startswith("dbg_")}
        with open("/root/problem/scratch/dev_dumps.pkl", "wb") as f:
            pickle.dump(dd, f)
    return out.reshape(2, 8192, 1024).astype(np.float32)
